# version 07
# baseline (speedup 1.0000x reference)
import numpy as np
import concourse.bass as bass
import concourse.mybir as mybir
from concourse.bass_utils import run_bass_kernel_spmd

F32 = mybir.dt.float32
BF16 = mybir.dt.bfloat16
AF = mybir.ActivationFunctionType
ALU = mybir.AluOpType
AX = mybir.AxisListType

B, S, H = 2, 4096, 4096
NH, D = 32, 128
INTER, IPAD = 11008, 11264
ITILES = IPAD // 128
IQ = ITILES // 4
TOK = 1024
NCORES = 8
S128 = float(1.0 / np.sqrt(128.0))
EPS = 1e-5
NEG = -1.0e9
KB = 1024
NTA, NTB = 20, 32
TBLK = 1024



def _alibi_slopes(n_heads):
    import math
    cp2 = 2 ** math.floor(math.log2(n_heads))
    base = 2 ** (-(2 ** -(math.log2(cp2) - 3)))
    slopes = base ** np.arange(1, cp2 + 1, dtype=np.float64)
    if cp2 != n_heads:
        extra_base = 2 ** (-(2 ** -(math.log2(2 * cp2) - 3)))
        n_rem = min(cp2, n_heads - cp2)
        extra = extra_base ** np.arange(1, 1 + 2 * n_rem, 2, dtype=np.float64)
        slopes = np.concatenate([slopes, extra])
    return slopes.astype(np.float32)


def _perm_for(g):
    rest = [c for c in range(8) if c not in (g, 7 - g)]
    return [g, 7 - g] + rest


def _prep_shared(w_attn, w_proj, gate_w, up_w, down_w):
    c = np.ascontiguousarray
    wq_t = c(w_attn[:, :H].reshape(H, 32, 128).transpose(1, 0, 2))
    wk_t = c(w_attn[:, H:2 * H].reshape(H, 32, 128).transpose(1, 0, 2))
    wv = c(w_attn[:, 2 * H:])
    def itile(w):
        wp = np.zeros((IPAD, H), dtype=np.float32)
        wp[:INTER] = w
        return c(wp.reshape(ITILES, 128, H).transpose(0, 2, 1))
    gw_t, uw_t = itile(gate_w), itile(up_w)
    dwp = np.zeros((IPAD, H), dtype=np.float32)
    dwp[:INTER] = c(down_w.T)
    dw_t = c(dwp.reshape(ITILES, 128, 32, 128).transpose(2, 0, 1, 3))
    return wq_t, wk_t, wv, c(w_proj), gw_t, uw_t, dw_t


def _prep_core(core, x):
    b, g = core // 4, core % 4
    c = np.ascontiguousarray
    order = _perm_for(g)
    perm = np.concatenate([np.arange(512 * ch, 512 * ch + 512) for ch in order])
    xb = c(x[b][perm])
    xown = c(xb[:TOK])
    slopes = _alibi_slopes(NH)
    kpos = perm.astype(np.float32)
    qpos = (kpos[:TOK] - 4096.0)[None, :]
    aA = np.empty((128, NH * NTA), dtype=np.float32)
    aB = np.empty((128, NH * NTB), dtype=np.float32)
    for h in range(NH):
        for t in range(NTA):
            ch = order[t // 4]
            kp = kpos[128 * t:128 * (t + 1)]
            if t >= 4 and ch > g:
                aA[:, h * NTA + t] = NEG
            else:
                aA[:, h * NTA + t] = slopes[h] * (kp - 4096.0)
        for t in range(NTB):
            ch = order[t // 4]
            kp = kpos[128 * t:128 * (t + 1)]
            if (t < 4 or t >= 8) and ch > 7 - g:
                aB[:, h * NTB + t] = NEG
            else:
                aB[:, h * NTB + t] = slopes[h] * (kp - 4096.0)
    j = np.arange(128)
    tri = np.where(j[:, None] > j[None, :], np.float32(NEG), np.float32(0.0))
    mtri = np.tile(tri, (1, 8)).astype(np.float32)
    return {
        "x_b": xb, "x_own": xown, "qpos": c(qpos),
        "alibi_a": aA, "alibi_b": aB, "mtri": mtri,
        "slopes": c(slopes[None, :]),
        "ones_row": np.ones((1, 128), dtype=np.float32),
        "ones_col": np.ones((128, 1), dtype=np.float32),
        "cc": np.tile(np.array([[EPS, -0.5, S128]], dtype=np.float32), (128, 1)),
        "ones128": np.ones((128, 128), dtype=np.float32),
    }



class Sched:
    ENGS = ("sp", "pool", "pe", "ve", "ac")

    def __init__(self):
        self.ops = {e: [] for e in self.ENGS}
        self.cnt = {}
        self.sems = {}

    def sem(self, name):
        self.cnt.setdefault(name, 0)
        return name

    def op(self, eng, fn, waits=(), inc=None, inc_by=1):
        waits = tuple((s, v) for (s, v) in waits if v > 0)
        if inc is not None:
            self.cnt[inc] = self.cnt.get(inc, 0) + inc_by
        target = self.cnt.get(inc, 0) if inc is not None else None
        sems = self.sems

        def run(e):
            for (s, v) in waits:
                e.wait_ge(sems[s], v)
            ins = fn(e)
            if inc is not None:
                ins.then_inc(sems[inc], inc_by)
        self.ops[eng].append(run)
        return target

    def dma(self, q, out, in_, waits=(), **kw):
        sem = self.sem("q_" + q)
        return self.op(q, lambda e: e.dma_start(out=out, in_=in_, **kw),
                       waits=waits, inc=sem, inc_by=16)

    def dmaT(self, q, out, in_, waits=()):
        sem = self.sem("q_" + q)
        return self.op(q, lambda e: e.dma_start_transpose(out, in_),
                       waits=waits, inc=sem, inc_by=16)

    def c(self, name):
        return self.cnt.get(name, 0)

    def barrier(self):
        return [("c_pe", self.c("c_pe")), ("c_ve", self.c("c_ve")),
                ("c_ac", self.c("c_ac")), ("q_sp", self.c("q_sp")),
                ("q_pool", self.c("q_pool"))]


def _dt_size(dtype):
    return {F32: 4, BF16: 2}[dtype]



def build(upto="E", dump=(), dbg=False):
    nc = bass.Bass("TRN2", target_bir_lowering=False)

    def dram_in(name, shape):
        return nc.dram_tensor(name, list(shape), F32, kind="ExternalInput").ap()

    x_b = dram_in("x_b", (S, H))
    x_own = dram_in("x_own", (TOK, H))
    wq_i = dram_in("wq", (32, H, 128))
    wk_i = dram_in("wk", (32, H, 128))
    wv_i = dram_in("wv", (H, H))
    bq_i = dram_in("bq", (128, 32))
    bk_i = dram_in("bk", (128, 32))
    bv_i = dram_in("bv_row", (1, H))
    wp_i = dram_in("wp", (H, H))
    bp_i = dram_in("bp_row", (1, H))
    gw_i = dram_in("gw", (ITILES, H, 128))
    uw_i = dram_in("uw", (ITILES, H, 128))
    dw_i = dram_in("dw", (32, ITILES, 128, 128))
    gs_i = dram_in("gs", (1, 1))
    us_i = dram_in("us", (1, 1))
    ds_i = dram_in("ds", (1, 1))
    slopes_i = dram_in("slopes", (1, NH))
    aA_i = dram_in("alibi_a", (128, NH * NTA))
    aB_i = dram_in("alibi_b", (128, NH * NTB))
    qpos_i = dram_in("qpos", (1, TOK))
    mtri_i = dram_in("mtri", (128, TOK))
    ones_row_i = dram_in("ones_row", (1, 128))
    ones_col_i = dram_in("ones_col", (128, 1))
    cc_i = dram_in("cc", (128, 3))
    ones128_i = dram_in("ones128", (128, 128))

    y_out = nc.dram_tensor("y", [TOK, H], F32, kind="ExternalOutput").ap()

    def scr(name, shape, dtype):
        kind = "ExternalOutput" if name in dump else "Internal"
        return nc.dram_tensor(name, list(shape), dtype, kind=kind).ap()

    xn_dram = scr("xn_dram", (S, H), BF16)
    xn_own_dram = scr("xn_own_dram", (TOK, H), BF16)
    qT_dram = scr("qT_dram", (H, TOK), BF16)
    kT_dram = scr("kT_dram", (H, S), BF16)
    v_dram = scr("v_dram", (NH, S, 128), BF16)
    aoT_dram = scr("aoT_dram", (H, TOK), BF16)
    proj_dram = scr("proj_dram", (TOK, H), BF16)
    dpA = scr("dpA", (H, TOK), BF16)
    dpB = scr("dpB", (H, TOK), BF16)
    sumT_dram = scr("sumT_dram", (H, TOK), BF16)
    dbg_p = scr("dbg_p", (48, 128, 512), F32) if dbg else None
    dbg_s = scr("dbg_s", (48, 128, 512), F32) if dbg else None

    base = (nc.sbuf_base + 31) & ~31
    avail = nc.sbuf_top - base

    def sb(name, off_kb, shape, dtype):
        off = int(off_kb * KB)
        end = off + int(np.prod(shape[1:])) * _dt_size(dtype)
        assert end <= avail, f"sbuf overflow {name}: {end / KB:.1f} > {avail / KB:.1f}"
        return nc.alloc_sbuf_tensor_at(name, list(shape), dtype, offset=base + off)

    bq_sc = sb("bq_sc", 0.0, [128, 32], F32)
    bk_sc = sb("bk_sc", 0.125, [128, 32], F32)
    ones_row = sb("ones_row", 0.25, [1, 128], F32)
    ones_bf = sb("ones_bf", 0.75, [128, 1], BF16)
    slopes_sb = sb("slopes_sb", 0.8125, [1, NH], F32)
    gs_sb = sb("gs_sb", 0.9375, [1, 1], F32)
    us_sb = sb("us_sb", 1.0, [1, 1], F32)
    ds_sb = sb("ds_sb", 1.0625, [1, 1], F32)
    usds_sb = sb("usds_sb", 1.125, [1, 1], F32)
    gs_col = sb("gs_col", 1.1875, [128, 1], F32)
    usds_col = sb("usds_col", 1.25, [128, 1], F32)
    col_buf = sb("col_buf", 1.3125, [128, 8], F32)
    kcm = sb("kcm", 1.375, [1, 16], F32)
    cc_sb = sb("cc_sb", 10.25, [128, 3], F32)
    ones_row_bf = sb("ones_row_bf", 9.8125, [1, 128], BF16)
    ones128 = sb("ones128", 10.5, [128, 128], BF16)
    qpos_sb = sb("qpos_sb", 1.4375, [1, TOK], F32)
    mtri_sb = sb("mtri_sb", 5.4375, [128, TOK], F32)
    bn_sb = sb("bn_sb", 9.4375, [128, 48], F32)
    PERM = 11.0

    xnT_blk = sb("xnT_blk", PERM, [128, 32 * TBLK], BF16)
    A2 = PERM + 64
    wqk_sb = [sb(f"wqk{i}", A2 + 8 * i, [128, 32 * 128], BF16) for i in range(2)]
    wvs_sb = [sb(f"wvs{i}", A2 + 16 + 32 * i, [128, 32 * 512], BF16) for i in range(2)]
    x_tile = [sb(f"x_tile{i}", A2 + 80 + 16 * i, [128, H], F32) for i in range(2)]
    xn_stage = [sb("xn_st0", A2 + 112, [128, H], BF16)]
    qk_stage = [sb(f"qk_st{i}", A2 + 120 + 1 * i, [128, 512], BF16) for i in range(2)]
    v_stage = [sb(f"v_st{i}", A2 + 122 + 1 * i, [128, 512], BF16) for i in range(2)]
    bv_bc = sb("bv_bc", A2 + 124, [128, H], BF16)

    BR = PERM
    qT_sb = sb("qT_sb", BR, [128, TOK], BF16)
    kT_sb = sb("kT_sb", BR + 2, [128, S], BF16)
    v_sb = sb("v_sb", BR + 10, [128, 32 * 128], BF16)
    qsq_sb = sb("qsq_sb", BR + 18, [128, TOK], BF16)
    ksq_sb = sb("ksq_sb", BR + 20, [128, S], BF16)
    hb_sb = sb("hb_sb", BR + 28, [128, TOK], F32)
    hbt_sb = sb("hbt_sb", BR + 32, [128, TOK], F32)
    aA_sb = sb("aA_sb", BR + 36, [128, NH * NTA], F32)
    aB_sb = sb("aB_sb", BR + 38.5, [128, NH * NTB], F32)
    s_buf = sb("s_buf", BR + 43, [128, 2048], F32)
    p_buf = sb("p_buf", BR + 51, [128, 2048], BF16)
    r_sb = sb("r_sb", BR + 55, [128, 512], F32)
    ao_stage = [sb(f"ao_st{i}", BR + 57 + 1 * i, [128, 512], BF16) for i in range(2)]
    qn_row = sb("qn_row", BR + 59, [1, TOK], F32)
    h_row = sb("h_row", BR + 63, [1, TOK], F32)
    h_row2 = sb("h_row2", BR + 67, [1, TOK], F32)
    r_row = sb("r_row", BR + 71, [1, 512], F32)

    CR = PERM
    aoT_sb = sb("aoT_sb", CR, [128, 32 * TOK], BF16)
    wpj_sb = [sb(f"wpj{i}", CR + 64 + 32 * i, [128, 32 * 512], BF16) for i in range(2)]
    bp_bc = sb("bp_bc", CR + 128, [128, H], F32)
    pj_stage = [sb(f"pj_st{i}", CR + 144 + 1 * i, [128, 512], BF16) for i in range(2)]

    DR = PERM
    xnT_own = sb("xnT_own", DR, [128, 32 * TOK], BF16)
    inter_sb = sb("inter_sb", DR + 64, [128, IQ * TOK], BF16)
    DS = DR + 108
    gw_sb = [sb(f"gw_sb{i}", DS + 8 * i, [128, 32 * 128], BF16) for i in range(2)]
    uw_sb = [sb(f"uw_sb{i}", DS + 16 + 8 * i, [128, 32 * 128], BF16) for i in range(2)]
    dw_sb = [sb(f"dw_sb{i}", DS + 32 + 5.5 * i, [128, IQ * 128], BF16) for i in range(2)]
    gsil_sb = sb("gsil_sb", DS + 43, [128, TOK], BF16)
    ubf_sb = sb("ubf_sb", DS + 45, [128, TOK], BF16)
    dp_tile = [sb(f"dp_t{i}", DS + 47 + 1 * i, [128, 512], BF16) for i in range(2)]
    d_stage = [sb(f"d_st{i}", DS + 49 + 1 * i, [128, 512], BF16) for i in range(2)]

    ER = PERM
    sumTT = [sb(f"sumTT{i}", ER + 8 * i, [128, H], BF16) for i in range(2)]
    x_e = [sb(f"x_e{i}", ER + 16 + 16 * i, [128, H], F32) for i in range(2)]
    pj_e = [sb(f"pj_e{i}", ER + 48 + 8 * i, [128, H], BF16) for i in range(2)]
    y1_sb = [sb(f"y1_{i}", ER + 64 + 16 * i, [128, H], F32) for i in range(2)]
    y_sb = [sb(f"y_{i}", ER + 96 + 16 * i, [128, H], F32) for i in range(2)]

    PS = [nc.alloc_psum_tensor(f"PS{i}", [128, 1024], F32) for i in range(4)]

    def psA(i):
        return PS[i // 2][:, (i % 2) * 512:(i % 2) * 512 + 512]

    sch = Sched()
    for s in ("q_sp", "q_pool", "c_pe", "c_ve", "c_ac"):
        sch.sem(s)

    for (dst, src) in [(bq_sc, bq_i), (bk_sc, bk_i), (ones_row, ones_row_i),
                       (slopes_sb, slopes_i), (gs_sb, gs_i), (us_sb, us_i),
                       (ds_sb, ds_i), (qpos_sb, qpos_i), (mtri_sb, mtri_i),
                       (cc_sb, cc_i)]:
        sch.dma("sp", dst[:, :], src)
    sch.dma("pool", ones_bf[:, :], ones_col_i)
    sch.dma("pool", ones128[:, :], ones128_i)
    ld0 = sch.c("q_sp")
    sch.op("ve", lambda e: e.tensor_scalar_mul(bq_sc[:, :], bq_sc[:, :], S128),
           waits=[("q_sp", ld0)], inc="c_ve")
    sch.op("ve", lambda e: e.tensor_scalar_mul(bk_sc[:, :], bk_sc[:, :], S128),
           inc="c_ve")
    sch.op("ve", lambda e: e.tensor_tensor(usds_sb[:, :], us_sb[:, :], ds_sb[:, :],
                                           ALU.mult), inc="c_ve")
    ve0 = sch.c("c_ve")
    bvld = sch.dma("pool", bv_bc[0:1, :], bv_i)
    orbld = sch.dma("pool", ones_row_bf[:, :], ones_row_i)
    for j in range(8):
        sch.op("pe", lambda e, j=j: e.matmul(PS[0][:, 0:512], ones_row_bf[:, :],
                                             bv_bc[0:1, j * 512:(j + 1) * 512],
                                             start=True, stop=True),
               waits=([("q_pool", orbld), ("c_ve", ve0), ("q_sp", ld0)] if j == 0
                      else [("c_ac", sch.c("c_ac"))]),
               inc="c_pe")
        g = sch.c("c_pe")
        sch.op("ac", lambda e, j=j: e.copy(bv_bc[:, j * 512:(j + 1) * 512],
                                           PS[0][:, 0:512]),
               waits=[("c_pe", g)], inc="c_ac")
    sch.op("pe", lambda e: e.matmul(PS[1][:, 0:1], ones_row[:, :], gs_sb[:, :],
                                    start=True, stop=True),
           waits=[("q_sp", ld0)], inc="c_pe")
    sch.op("pe", lambda e: e.matmul(PS[1][:, 1:2], ones_row[:, :], usds_sb[:, :],
                                    start=True, stop=True),
           waits=[("c_ve", ve0)], inc="c_pe")
    g = sch.c("c_pe")
    sch.op("ac", lambda e: e.copy(gs_col[:, :], PS[1][:, 0:1]),
           waits=[("c_pe", g)], inc="c_ac")
    sch.op("ac", lambda e: e.copy(usds_col[:, :], PS[1][:, 1:2]), inc="c_ac")
    ac0 = sch.c("c_ac")

    lastLN = {"aff": [0, 0], "st": 0}

    def ln_tile(src_ap, dst_rows_ap, par):
        xt, xs = x_tile[par], xn_stage[0]
        ld = sch.dma("sp", xt[:, :], src_ap,
                     waits=[("c_ve", lastLN["aff"][par])])
        for j in range(8):
            sch.op("ve", lambda e, j=j: e.bn_stats(bn_sb[:, j * 6:j * 6 + 6],
                                                   xt[:, j * 512:(j + 1) * 512]),
                   waits=([("q_sp", ld)] if j == 0 else []),
                   inc=("c_ve" if j == 7 else None))
        vbn = sch.c("c_ve")
        sch.op("ve", lambda e: e.bn_aggr(col_buf[:, 0:2], bn_sb[:, :]),
               waits=[("c_ve", vbn)], inc="c_ve")
        vagg = sch.c("c_ve")
        sch.op("ac", lambda e: e.activation(col_buf[:, 2:3], col_buf[:, 1:2],
                                            AF.Ln, bias=cc_sb[:, 0:1]),
               waits=[("c_ve", vagg)], inc="c_ac")
        a1 = sch.c("c_ac")
        sch.op("ac", lambda e: e.activation(col_buf[:, 3:4], col_buf[:, 2:3],
                                            AF.Exp, scale=cc_sb[:, 1:2]),
               waits=[("c_ac", a1)], inc="c_ac")
        a2 = sch.c("c_ac")
        sch.op("ve", lambda e: e.tensor_scalar(col_buf[:, 4:5], col_buf[:, 0:1],
                                               col_buf[:, 3:4], -1.0,
                                               ALU.mult, ALU.mult),
               waits=[("c_ac", a2)], inc="c_ve")
        vB = sch.c("c_ve")
        sch.op("ve", lambda e: e.tensor_scalar(xs[:, :], xt[:, :],
                                               col_buf[:, 3:4], col_buf[:, 4:5],
                                               ALU.mult, ALU.add),
               waits=[("c_ve", vB), ("q_sp", lastLN["st"])], inc="c_ve")
        aff = sch.c("c_ve")
        st = sch.dma("sp", dst_rows_ap, xs[:, :], waits=[("c_ve", aff)])
        lastLN["aff"][par], lastLN["st"] = aff, st
        return st

    st_free = {"qk": [0, 0], "v": [0, 0]}
    ps_free_qk = [ac0, ac0]
    ps_free_v = [ac0, ac0]
    qkstrip_free = [0, 0]
    vstrip_free = [0, 0]
    xnT_free = [0]
    qk_pp = [0]
    qk_sp = [0]

    def qk_part(w_in, bias_sb, dst_dram, dst_col0, n_tcols, ct, tr_last):
        par = ct % 2
        ld = sch.dma("pool", wqk_sb[par][:, :].rearrange("p (g c) -> p g c", g=32),
                     w_in.rearrange("(g p) c -> p g c", p=128),
                     waits=[("c_pe", qkstrip_free[par])])
        for th in range(n_tcols):
            pp = qk_pp[0]
            qk_pp[0] ^= 1
            spar = qk_sp[0]
            qk_sp[0] = (qk_sp[0] + 1) % 2
            wmm = [("q_pool", ld), ("q_sp", tr_last), ("c_ve", ps_free_qk[pp])]
            for hcc in range(32):
                sch.op("pe", lambda e, hcc=hcc, pp=pp, par=par, th=th:
                       e.matmul(psA(pp),
                                wqk_sb[par][:, hcc * 128:(hcc + 1) * 128],
                                xnT_blk[:, hcc * TBLK + th * 512:
                                        hcc * TBLK + th * 512 + 512],
                                start=(hcc == 0), stop=(hcc == 31)),
                       waits=(wmm if hcc == 0 else []),
                       inc=("c_pe" if hcc == 31 else None))
            gpe = sch.c("c_pe")
            sch.op("ve", lambda e, pp=pp, spar=spar, ct=ct:
                   e.tensor_scalar(qk_stage[spar][:, :], psA(pp),
                                   S128, bias_sb[:, ct:ct + 1],
                                   ALU.mult, ALU.add),
                   waits=[("c_pe", gpe), ("q_sp", st_free["qk"][spar])],
                   inc="c_ve")
            ps_free_qk[pp] = sch.c("c_ve")
            aev = sch.c("c_ve")
            st_free["qk"][spar] = sch.dma(
                "sp", dst_dram[ct * 128:(ct + 1) * 128,
                               dst_col0 + th * 512:dst_col0 + th * 512 + 512],
                qk_stage[spar][:, :], waits=[("c_ve", aev)])
        qkstrip_free[par] = sch.c("c_pe")

    def ln_block(tb):
        st_last = 0
        for tt in range(8):
            rows = tb * TBLK + tt * 128
            st_last = ln_tile(x_b[rows:rows + 128, :],
                              xn_dram[rows:rows + 128, :], tt % 2)
        return st_last

    def phaseA():
        ln_st = [0] * 5
        for tb in range(4):
            if tb == 0:
                ln_st[0] = ln_block(0)
            st_last = ln_st[tb]
            tr_last = 0
            for hc in range(32):
                w = []
                if hc == 0:
                    w = [("q_sp", st_last)]
                    if xnT_free[0]:
                        w.append(("c_pe", xnT_free[0]))
                tr_last = sch.dmaT("sp", xnT_blk[:, hc * TBLK:(hc + 1) * TBLK],
                                   xn_dram[tb * TBLK:(tb + 1) * TBLK,
                                           hc * 128:(hc + 1) * 128], waits=w)
            for ct in range(32):
                qk_part(wk_i[ct], bk_sc, kT_dram, tb * TBLK, TBLK // 512, ct,
                        tr_last)
                if ct % 4 == 0:
                    tt = ct // 4
                    nb = tb + 1
                    if nb < 4:
                        rows = nb * TBLK + tt * 128
                        ln_st[nb] = ln_tile(x_b[rows:rows + 128, :],
                                            xn_dram[rows:rows + 128, :], tt % 2)
                    else:
                        rows = tt * 128
                        ln_st[nb] = ln_tile(x_own[rows:rows + 128, :],
                                            xn_own_dram[rows:rows + 128, :],
                                            tt % 2)
            if tb == 0:
                for ct in range(32):
                    qk_part(wq_i[ct], bq_sc, qT_dram, 0, TOK // 512, ct, tr_last)
            for vc in range(8):
                par = vc % 2
                ldv = sch.dma("pool",
                              wvs_sb[par][:, :].rearrange("p (g c) -> p g c", g=32),
                              wv_i[:, vc * 512:(vc + 1) * 512].rearrange(
                                  "(g p) c -> p g c", p=128),
                              waits=[("c_pe", vstrip_free[par])])
                for tt in range(TBLK // 128):
                    pp = 2 + (tt % 2)
                    spar = tt % 2
                    wmm = [("q_pool", ldv), ("c_ve", ps_free_v[pp - 2]),
                           ("c_ac", ac0)]
                    for hcc in range(32):
                        sch.op("pe", lambda e, hcc=hcc, pp=pp, par=par, tt=tt:
                               e.matmul(psA(pp),
                                        xnT_blk[:, hcc * TBLK + tt * 128:
                                                hcc * TBLK + tt * 128 + 128],
                                        wvs_sb[par][:, hcc * 512:(hcc + 1) * 512],
                                        start=(hcc == 0), stop=(hcc == 31)),
                               waits=(wmm if hcc == 0 else []),
                               inc=("c_pe" if hcc == 31 else None))
                    gpe = sch.c("c_pe")
                    sch.op("ve", lambda e, pp=pp, spar=spar, vc=vc:
                           e.tensor_tensor(v_stage[spar][:, :], psA(pp),
                                           bv_bc[:, vc * 512:(vc + 1) * 512],
                                           ALU.add),
                           waits=[("c_pe", gpe), ("q_sp", st_free["v"][spar])],
                           inc="c_ve")
                    ps_free_v[pp - 2] = sch.c("c_ve")
                    vev = sch.c("c_ve")
                    rows = tb * TBLK + tt * 128
                    last = 0
                    for i in range(4):
                        h = vc * 4 + i
                        last = sch.dma("sp", v_dram[h, rows:rows + 128, :],
                                       v_stage[spar][:, i * 128:(i + 1) * 128],
                                       waits=([("c_ve", vev)] if i == 0 else []))
                    st_free["v"][spar] = last
                vstrip_free[par] = sch.c("c_pe")
            xnT_free[0] = sch.c("c_pe")

    def phaseB():
        bar = sch.barrier()
        la = sch.dma("sp", aA_sb[:, :], aA_i, waits=bar)
        lb = sch.dma("sp", aB_sb[:, :], aB_i)
        p_free = [0, 0]
        s_free = {0: 0, 1: 0}
        ps_free = [0, 0]
        bcast_ac = [0]
        av_free = [0, 0]
        ao_free = [0, 0]
        prev_pe = [0]
        norm_ve = [0, 0]
        nslot = [PS[3][:, 0:512], PS[3][:, 512:1024]]

        for h in range(NH):
            wq_ = [("c_pe", prev_pe[0])] if prev_pe[0] else [("q_sp", lb)]
            lq = sch.dma("sp", qT_sb[:, :], qT_dram[h * 128:(h + 1) * 128, :],
                         waits=wq_)
            lk = sch.dma("sp", kT_sb[:, :], kT_dram[h * 128:(h + 1) * 128, :])
            lv = sch.dma("sp", v_sb[:, :].rearrange("p (g d) -> p g d", g=32),
                         v_dram[h].rearrange("(g p) d -> p g d", p=128))
            sch.op("ac", lambda e: e.activation(qsq_sb[:, :], qT_sb[:, :],
                                                AF.Square),
                   waits=[("q_sp", lq)], inc="c_ac")
            aq = sch.c("c_ac")
            sch.op("ac", lambda e: e.activation(ksq_sb[:, :], kT_sb[:, :],
                                                AF.Square),
                   waits=[("q_sp", lk)], inc="c_ac")
            ak = sch.c("c_ac")
            for j in range(2):
                sl = j % 2
                sch.op("pe", lambda e, j=j, sl=sl: e.matmul(
                    nslot[sl], ones128[:, :], qsq_sb[:, j * 512:(j + 1) * 512],
                    start=True, stop=True),
                    waits=[("c_ac", aq), ("c_ve", norm_ve[sl])], inc="c_pe")
                g = sch.c("c_pe")
                sch.op("ve", lambda e, j=j, sl=sl: e.tensor_copy(
                    qn_row[0:1, j * 512:(j + 1) * 512], nslot[sl][0:1, :]),
                    waits=[("c_pe", g)], inc="c_ve")
                norm_ve[sl] = sch.c("c_ve")
            for j in range(8):
                sl = j % 2
                sch.op("pe", lambda e, j=j, sl=sl: e.matmul(
                    nslot[sl], ones128[:, :], ksq_sb[:, j * 512:(j + 1) * 512],
                    start=True, stop=True),
                    waits=[("c_ac", ak), ("c_ve", norm_ve[sl])], inc="c_pe")
                g = sch.c("c_pe")
                sch.op("ve", lambda e, j=j, sl=sl: e.reduce_max(
                    kcm[0:1, j:j + 1], nslot[sl][0:1, :], axis=AX.X),
                    waits=[("c_pe", g)], inc="c_ve")
                norm_ve[sl] = sch.c("c_ve")
            sch.op("ve", lambda e: e.reduce_max(kcm[0:1, 8:9], kcm[0:1, 0:8],
                                                axis=AX.X),
                   waits=[("c_ve", max(norm_ve))], inc="c_ve")
            v1 = sch.c("c_ve")
            sch.op("ve", lambda e: e.tensor_scalar_mul(kcm[0:1, 9:10],
                                                       kcm[0:1, 8:9], -0.5),
                   waits=[("c_ve", v1)], inc="c_ve")
            v2 = sch.c("c_ve")
            sch.op("ve", lambda e: e.tensor_scalar(h_row[0:1, :], qn_row[0:1, :],
                                                   -0.5, kcm[0:1, 9:10],
                                                   ALU.mult, ALU.add),
                   waits=[("c_ve", v2)], inc="c_ve")
            v3 = sch.c("c_ve")
            hh = h
            sch.op("ve", lambda e, hh=hh: e.tensor_scalar_mul(
                kcm[0:1, 10:11], slopes_sb[0:1, hh:hh + 1], -1.0),
                waits=[("c_ve", v3)], inc="c_ve")
            v4 = sch.c("c_ve")
            sch.op("ve", lambda e: e.scalar_tensor_tensor(
                h_row2[0:1, :], qpos_sb[0:1, :], kcm[0:1, 10:11], h_row[0:1, :],
                ALU.mult, ALU.add),
                waits=[("c_ve", v4)], inc="c_ve")
            v5 = sch.c("c_ve")
            norm_ve[0] = max(norm_ve[0], v5)
            for j in range(2):
                sch.op("pe", lambda e, j=j: e.matmul(
                    PS[3][:, 512:1024], ones_row[:, :],
                    h_row2[0:1, j * 512:(j + 1) * 512], start=True, stop=True),
                    waits=[("c_ve", v5), ("c_ac", sch.c("c_ac"))], inc="c_pe")
                g = sch.c("c_pe")
                sch.op("ac", lambda e, j=j: e.copy(
                    hb_sb[:, j * 512:(j + 1) * 512], PS[3][:, 512:1024]),
                    waits=[("c_pe", g)], inc="c_ac")
            ahb = sch.c("c_ac")
            sch.op("ve", lambda e: e.tensor_tensor(hbt_sb[:, :], hb_sb[:, :],
                                                   mtri_sb[:, :], ALU.add),
                   waits=[("c_ac", ahb)], inc="c_ve")
            vhbt = sch.c("c_ve")
            bcast_ac[0] = ahb

            gidx = [0]

            def chunk_loop(q0, ntiles, diag0, ab_sb, ab_base, avp):
                SL = [PS[0], PS[2]]
                npairs = ntiles // 2
                pending = []
                s_free_at = {}
                memset_at = {}

                def emit_denav(u2):
                    for half in range(2):
                        g2 = 2 * u2 + half
                        pv = p_buf[:, (g2 % 4) * 512:(g2 % 4) * 512 + 512]
                        wden = [("c_ac", s_free_at[g2])]
                        if g2 in memset_at:
                            wden.append(("c_ve", memset_at[g2]))
                        if g2 == 0:
                            wden.append(("c_ve", av_free[avp]))
                            wden.append(("c_ve", max(norm_ve)))
                            wden.append(("c_ac", bcast_ac[0]))
                        sch.op("pe", lambda e, pv=pv, g2=g2:
                               e.matmul(PS[3][:, avp * 512:avp * 512 + 512],
                                        ones128[:, :], pv,
                                        start=(g2 == 0), stop=(g2 == ntiles - 1)),
                               waits=wden)
                        sch.op("pe", lambda e, pv=pv, g2=g2:
                               e.matmul(psA(2 + avp),
                                        v_sb[:, g2 * 128:(g2 + 1) * 128], pv,
                                        start=(g2 == 0), stop=(g2 == ntiles - 1)),
                               inc=("c_pe" if half == 1 else None))
                    p_free[u2 % 2] = sch.c("c_pe")

                for u in range(npairs):
                    su = gidx[0] % 2
                    gidx[0] += 1
                    g0, g1 = 2 * u, 2 * u + 1
                    diag = diag0 <= g0 < diag0 + 4
                    for half, g_ in ((0, g0), (1, g1)):
                        m = g_ - diag0
                        lo = 128 * m if (diag and m > 0) else 0
                        wsc = []
                        if half == 0:
                            wsc = [("c_ve", ps_free[su])]
                            if g0 == 0:
                                wsc += [("c_ve", vhbt), ("q_sp", lv), ("q_sp", la)]
                        sch.op("pe", lambda e, g_=g_, su=su, q0=q0, lo=lo, half=half:
                               e.matmul(SL[su][:, half * 512 + lo:half * 512 + 512],
                                        kT_sb[:, g_ * 128:(g_ + 1) * 128],
                                        qT_sb[:, q0 + lo:q0 + 512],
                                        start=True, stop=True),
                               waits=wsc,
                               inc=("c_pe" if half == 1 else None))
                    gpe = sch.c("c_pe")
                    if len(pending) >= 2:
                        emit_denav(pending.pop(0))
                    sslot = u % 2
                    wv_ = [("c_pe", gpe), ("c_ac", s_free.get(sslot, 0))]
                    if not diag:
                        hb2 = hb_sb[:, q0:q0 + 512].rearrange(
                            "p (o q) -> p o q", o=1).broadcast_to([128, 2, 512])
                        sch.op("ve", lambda e, su=su, sslot=sslot, hb2=hb2:
                               e.tensor_tensor(
                                   s_buf[:, sslot * 1024:sslot * 1024 + 1024]
                                   .rearrange("p (o q) -> p o q", o=2),
                                   SL[su][:, :].rearrange("p (o q) -> p o q", o=2),
                                   hb2, ALU.add),
                               waits=wv_, inc="c_ve")
                    else:
                        for half, g_ in ((0, g0), (1, g1)):
                            m = g_ - diag0
                            lo = 128 * m
                            sch.op("ve", lambda e, su=su, sslot=sslot, q0=q0,
                                   lo=lo, half=half:
                                   e.tensor_tensor(
                                       s_buf[:, sslot * 1024 + half * 512 + lo:
                                             sslot * 1024 + half * 512 + lo + 128],
                                       SL[su][:, half * 512 + lo:half * 512 + lo + 128],
                                       hbt_sb[:, q0 + lo:q0 + lo + 128], ALU.add),
                                   waits=(wv_ if half == 0 else []),
                                   inc=("c_ve" if (half == 1 and m == 3) else None))
                            if m < 3:
                                sch.op("ve", lambda e, su=su, sslot=sslot, q0=q0,
                                       lo=lo, half=half:
                                       e.tensor_tensor(
                                           s_buf[:, sslot * 1024 + half * 512 + lo + 128:
                                                 sslot * 1024 + half * 512 + 512],
                                           SL[su][:, half * 512 + lo + 128:
                                                  half * 512 + 512],
                                           hb_sb[:, q0 + lo + 128:q0 + 512], ALU.add),
                                       inc=("c_ve" if half == 1 else None))
                    gve = sch.c("c_ve")
                    ps_free[su] = gve
                    for half, g_ in ((0, g0), (1, g1)):
                        m = g_ - diag0
                        lo = 128 * m if (diag and m > 0) else 0
                        pslot = g_ % 4
                        if diag and m > 0:
                            sch.op("ve", lambda e, pslot=pslot, m=m:
                                   e.memset(p_buf[:, pslot * 512:pslot * 512 + 128 * m],
                                            0.0),
                                   waits=[("c_pe", p_free[(g_ // 2) % 2])],
                                   inc="c_ve")
                            memset_at[g_] = sch.c("c_ve")
                        col = ab_base + g_
                        sch.op("ac", lambda e, pslot=pslot, sslot=sslot, half=half,
                               lo=lo, col=col:
                               e.activation(p_buf[:, pslot * 512 + lo:pslot * 512 + 512],
                                            s_buf[:, sslot * 1024 + half * 512 + lo:
                                                  sslot * 1024 + half * 512 + 512],
                                            AF.Exp, bias=ab_sb[:, col:col + 1]),
                               waits=[("c_ve", gve),
                                      ("c_pe", p_free[(g_ // 2) % 2])], inc="c_ac")
                        s_free_at[g_] = sch.c("c_ac")
                    s_free[sslot] = sch.c("c_ac")
                    pending.append(u)
                while pending:
                    emit_denav(pending.pop(0))
                gend = sch.c("c_pe")
                sch.op("ve", lambda e, avp=avp: e.reciprocal(
                    r_sb[:, :], PS[3][:, avp * 512:avp * 512 + 512]),
                    waits=[("c_pe", gend)], inc="c_ve")
                norm_ve[avp] = max(norm_ve[avp], sch.c("c_ve"))
                aop = avp
                sch.op("ve", lambda e, avp=avp, aop=aop: e.tensor_tensor(
                    ao_stage[aop][:, :], psA(2 + avp), r_sb[:, :], ALU.mult),
                    waits=[("q_sp", ao_free[aop])], inc="c_ve")
                av_free[avp] = sch.c("c_ve")
                ao_free[aop] = sch.dma(
                    "sp", aoT_dram[h * 128:(h + 1) * 128, q0:q0 + 512],
                    ao_stage[aop][:, :], waits=[("c_ve", av_free[avp])])

            chunk_loop(0, NTA, 0, aA_sb, h * NTA, 0)
            chunk_loop(512, NTB, 4, aB_sb, h * NTB, 1)
            prev_pe[0] = sch.c("c_pe")

    def phaseC():
        bar = sch.barrier()
        lao = sch.dma("sp", aoT_sb[:, :].rearrange("p (g t) -> p g t", g=32),
                      aoT_dram.rearrange("(g p) t -> p g t", p=128), waits=bar)
        lbp = sch.dma("sp", bp_bc[0:1, :], bp_i)
        for j in range(8):
            sch.op("pe", lambda e, j=j: e.matmul(
                PS[0][:, 0:512], ones_row[:, :],
                bp_bc[0:1, j * 512:(j + 1) * 512], start=True, stop=True),
                waits=([("q_sp", lbp)] + bar[:3] if j == 0
                       else [("c_ac", sch.c("c_ac"))]),
                inc="c_pe")
            g = sch.c("c_pe")
            sch.op("ac", lambda e, j=j: e.copy(bp_bc[:, j * 512:(j + 1) * 512],
                                               PS[0][:, 0:512]),
                   waits=[("c_pe", g)], inc="c_ac")
        acb = sch.c("c_ac")
        pj_pe = [0, 0]
        pj_ve = [0, 0]
        pj_st = [0, 0]
        for cb in range(8):
            par = cb % 2
            wld = [("c_pe", pj_pe[par])] if pj_pe[par] else list(bar)
            ld = sch.dma("pool",
                         wpj_sb[par][:, :].rearrange("p (g c) -> p g c", g=32),
                         wp_i[:, cb * 512:(cb + 1) * 512].rearrange(
                             "(g p) c -> p g c", p=128), waits=wld)
            for t1 in range(8):
                pp = (cb * 8 + t1) % 2
                spar = (cb * 8 + t1) % 2
                wmm = [("q_pool", ld), ("q_sp", lao), ("c_ve", pj_ve[pp]),
                       ("c_ac", acb)]
                for hcc in range(32):
                    sch.op("pe", lambda e, hcc=hcc, pp=pp, par=par, t1=t1:
                           e.matmul(psA(pp),
                                    aoT_sb[:, hcc * TOK + t1 * 128:
                                           hcc * TOK + t1 * 128 + 128],
                                    wpj_sb[par][:, hcc * 512:(hcc + 1) * 512],
                                    start=(hcc == 0), stop=(hcc == 31)),
                           waits=(wmm if hcc == 0 else []),
                           inc=("c_pe" if hcc == 31 else None))
                pj_pe[par] = sch.c("c_pe")
                g = sch.c("c_pe")
                sch.op("ve", lambda e, pp=pp, spar=spar, cb=cb:
                       e.tensor_tensor(pj_stage[spar][:, :], psA(pp),
                                       bp_bc[:, cb * 512:(cb + 1) * 512],
                                       ALU.add),
                       waits=[("c_pe", g), ("q_sp", pj_st[spar])], inc="c_ve")
                pj_ve[pp] = sch.c("c_ve")
                vv = sch.c("c_ve")
                pj_st[spar] = sch.dma(
                    "sp", proj_dram[t1 * 128:(t1 + 1) * 128,
                                    cb * 512:(cb + 1) * 512],
                    pj_stage[spar][:, :], waits=[("c_ve", vv)])

    gu_pe = [0, 0]
    ga_ac = [0, 0]
    ga_ve = [0, 0]
    ga_tt = [0]
    int_free = [0]
    dw_pe = [0, 0]
    pd_evac = [0, 0]
    dst_free = [0, 0]
    dp_ld_free = [0, 0]

    def phaseD_xn(bar):
        tr = 0
        for hc in range(32):
            tr = sch.dmaT("sp", xnT_own[:, hc * TOK:(hc + 1) * TOK],
                          xn_own_dram[:, hc * 128:(hc + 1) * 128],
                          waits=(bar if hc == 0 else []))
        return tr

    def phaseDa(q, trx, mm_extra=()):
        for it in range(IQ):
            gt = q * IQ + it
            par = it % 2
            wld = [("c_pe", gu_pe[par])] if gu_pe[par] else []
            if it < 2:
                wld += list(mm_extra)
            lg = sch.dma("pool",
                         gw_sb[par][:, :].rearrange("p (g c) -> p g c", g=32),
                         gw_i[gt].rearrange("(g p) c -> p g c", p=128),
                         waits=wld)
            lu = sch.dma("pool",
                         uw_sb[par][:, :].rearrange("p (g c) -> p g c", g=32),
                         uw_i[gt].rearrange("(g p) c -> p g c", p=128))
            ppg = 0 if par == 0 else 2
            ppu = 1 if par == 0 else 3
            wmm = [("q_pool", lu), ("q_sp", trx),
                   ("c_ac", ga_ac[par]), ("c_ve", ga_ve[par])]
            if it < 2:
                wmm += list(mm_extra)
            first = [True]
            for th in range(2):
                for hcc in range(32):
                    w = wmm if first[0] else []
                    first[0] = False
                    sch.op("pe", lambda e, hcc=hcc, ppg=ppg, par=par, th=th:
                           e.matmul(PS[ppg][:, th * 512:th * 512 + 512],
                                    gw_sb[par][:, hcc * 128:(hcc + 1) * 128],
                                    xnT_own[:, hcc * TOK + th * 512:
                                            hcc * TOK + th * 512 + 512],
                                    start=(hcc == 0), stop=(hcc == 31)),
                           waits=w)
            for th in range(2):
                for hcc in range(32):
                    sch.op("pe", lambda e, hcc=hcc, ppu=ppu, par=par, th=th:
                           e.matmul(PS[ppu][:, th * 512:th * 512 + 512],
                                    uw_sb[par][:, hcc * 128:(hcc + 1) * 128],
                                    xnT_own[:, hcc * TOK + th * 512:
                                            hcc * TOK + th * 512 + 512],
                                    start=(hcc == 0), stop=(hcc == 31)),
                           waits=[],
                           inc=("c_pe" if (th == 1 and hcc == 31) else None))
            gu_pe[par] = sch.c("c_pe")
            g = sch.c("c_pe")
            sch.op("ac", lambda e, ppg=ppg: e.activation(
                gsil_sb[:, :], PS[ppg][:, :], AF.Silu, scale=gs_col[:, 0:1]),
                waits=[("c_pe", g), ("c_ve", ga_tt[0])], inc="c_ac")
            ga_ac[par] = sch.c("c_ac")
            a1 = sch.c("c_ac")
            sch.op("ve", lambda e, ppu=ppu: e.tensor_scalar_mul(
                ubf_sb[:, :], PS[ppu][:, :], usds_col[:, 0:1]),
                waits=[("c_pe", g)], inc="c_ve")
            u1 = sch.c("c_ve")
            sch.op("ve", lambda e, it=it: e.tensor_tensor(
                inter_sb[:, it * TOK:(it + 1) * TOK], gsil_sb[:, :],
                ubf_sb[:, :], ALU.mult),
                waits=[("c_ve", u1), ("c_ac", a1), ("c_pe", int_free[0])],
                inc="c_ve")
            ga_ve[par] = sch.c("c_ve")
            ga_tt[0] = sch.c("c_ve")

    def phaseDb(q, src_dp, dst_dp, mm_extra=()):
        inter_ready = sch.c("c_ve")
        db_qsp = sch.c("q_sp")
        for ht in range(32):
            par = ht % 2
            wld = [("c_pe", dw_pe[par])] if dw_pe[par] else []
            ld = sch.dma("pool",
                         dw_sb[par][:, :].rearrange("p (it hh) -> p it hh", it=IQ),
                         dw_i[ht, q * IQ:(q + 1) * IQ].rearrange(
                             "it p hh -> p it hh"),
                         waits=wld)
            for th in range(2):
                pp = (ht * 2 + th) % 2
                spar = (ht * 2 + th) % 2
                ldp = 0
                if src_dp is not None:
                    w = [("c_ve", dp_ld_free[spar])]
                    if ht == 0 and th == 0:
                        w.append(("q_sp", db_qsp))
                    ldp = sch.dma("sp", dp_tile[spar][:, :],
                                  src_dp[ht * 128:(ht + 1) * 128,
                                         th * 512:th * 512 + 512], waits=w)
                wmm = [("q_pool", ld), ("c_ve", pd_evac[pp]),
                       ("c_ve", inter_ready)]
                if ht == 0 and th == 0:
                    wmm += list(mm_extra)
                for it in range(IQ):
                    sch.op("pe", lambda e, it=it, pp=pp, par=par, th=th:
                           e.matmul(psA(pp),
                                    dw_sb[par][:, it * 128:(it + 1) * 128],
                                    inter_sb[:, it * TOK + th * 512:
                                             it * TOK + th * 512 + 512],
                                    start=(it == 0), stop=(it == IQ - 1)),
                           waits=(wmm if it == 0 else []),
                           inc=("c_pe" if it == IQ - 1 else None))
                dw_pe[par] = sch.c("c_pe")
                g = sch.c("c_pe")
                rows = slice(ht * 128, (ht + 1) * 128)
                cols = slice(th * 512, th * 512 + 512)
                dd = dst_dp if dst_dp is not None else sumT_dram
                if src_dp is None:
                    sch.op("ve", lambda e, pp=pp, spar=spar: e.tensor_copy(
                        d_stage[spar][:, :], psA(pp)),
                        waits=[("c_pe", g), ("q_sp", dst_free[spar])],
                        inc="c_ve")
                else:
                    sch.op("ve", lambda e, pp=pp, spar=spar:
                           e.scalar_tensor_tensor(
                               d_stage[spar][:, :], psA(pp), 1.0,
                               dp_tile[spar][:, :], ALU.mult, ALU.add),
                           waits=[("c_pe", g), ("q_sp", ldp),
                                  ("q_sp", dst_free[spar])], inc="c_ve")
                    dp_ld_free[spar] = sch.c("c_ve")
                pd_evac[pp] = sch.c("c_ve")
                dst_free[spar] = sch.dma("sp", dd[rows, cols],
                                         d_stage[spar][:, :],
                                         waits=[("c_ve", pd_evac[pp])])

    def phaseE():
        bar = sch.barrier()
        e_ve = [0, 0]
        e_st = [0, 0]
        for tt in range(8):
            par = tt % 2
            w0 = list(bar) if tt == 0 else []
            ltr = sch.dmaT("sp", sumTT[par][:, :],
                           sumT_dram[:, tt * 128:(tt + 1) * 128],
                           waits=w0 + ([("c_ve", e_ve[par])] if e_ve[par] else []))
            lx = sch.dma("sp", x_e[par][:, :], x_own[tt * 128:(tt + 1) * 128, :])
            lp = sch.dma("sp", pj_e[par][:, :],
                         proj_dram[tt * 128:(tt + 1) * 128, :])
            sch.op("ve", lambda e, par=par: e.tensor_tensor(
                y1_sb[par][:, :], x_e[par][:, :], pj_e[par][:, :], ALU.add),
                waits=[("q_sp", lp), ("c_ve", e_ve[par])], inc="c_ve")
            v1 = sch.c("c_ve")
            sch.op("ve", lambda e, par=par: e.tensor_tensor(
                y_sb[par][:, :], y1_sb[par][:, :], sumTT[par][:, :], ALU.add),
                waits=[("c_ve", v1), ("q_sp", e_st[par])], inc="c_ve")
            e_ve[par] = sch.c("c_ve")
            e_st[par] = sch.dma("sp", y_out[tt * 128:(tt + 1) * 128, :],
                                y_sb[par][:, :], waits=[("c_ve", e_ve[par])])
        total = sch.c("q_sp")
        sch.op("sp", lambda e: e.nop(), waits=[("q_sp", total)])

    order = ["0", "A", "Aown", "B", "C", "D", "E"]
    stop = order.index(upto)
    if stop >= 1:
        phaseA()
    if stop >= 3:
        phaseB()
    if stop >= 4:
        phaseC()
    if stop >= 5:
        cbar = [("c_pe", sch.c("c_pe")), ("c_ac", sch.c("c_ac")),
                ("c_ve", sch.c("c_ve"))]
        trx = phaseD_xn(sch.barrier())
        phaseDa(0, trx, mm_extra=cbar)
        phaseDb(0, None, dpA, mm_extra=cbar)
        for q, (s_, d_) in enumerate([(dpA, dpB), (dpB, dpA), (dpA, None)],
                                     start=1):
            int_free[0] = sch.c("c_pe")
            phaseDa(q, trx,
                    mm_extra=[("c_ve", pd_evac[0]), ("c_ve", pd_evac[1])])
            phaseDb(q, s_, d_)
    if stop >= 6:
        phaseE()
    else:
        sch.op("sp", lambda e: e.nop(), waits=[("q_sp", sch.c("q_sp"))])
        sch.op("pool", lambda e: e.nop(), waits=[("q_pool", sch.c("q_pool"))])

    import contextlib
    with contextlib.ExitStack() as stack:
        for name in list(sch.cnt):
            sch.sems[name] = stack.enter_context(nc.semaphore(name))
        with nc.Block() as block:
            @block.sync
            def _(e):
                for f in sch.ops["sp"]:
                    f(e)

            @block.gpsimd
            def _(e):
                for f in sch.ops["pool"]:
                    f(e)

            @block.tensor
            def _(e):
                for f in sch.ops["pe"]:
                    f(e)

            @block.vector
            def _(e):
                for f in sch.ops["ve"]:
                    f(e)

            @block.scalar
            def _(e):
                for f in sch.ops["ac"]:
                    f(e)

    return nc



def kernel(L0_attn_hidden_states, L0_attn_w_attn, L0_attn_b_attn,
           L0_attn_w_proj, L0_attn_b_proj, L0_ffn_gate_w, L0_ffn_up_w,
           L0_ffn_down_w, L0_ffn_gate_scale, L0_ffn_up_scale,
           L0_ffn_down_scale):
    c = np.ascontiguousarray
    x = np.asarray(L0_attn_hidden_states, dtype=np.float32)
    w_attn = np.asarray(L0_attn_w_attn, dtype=np.float32)
    b_attn = np.asarray(L0_attn_b_attn, dtype=np.float32)
    wq_t, wk_t, wv, wp, gw_t, uw_t, dw_t = _prep_shared(
        w_attn, np.asarray(L0_attn_w_proj, dtype=np.float32),
        np.asarray(L0_ffn_gate_w, dtype=np.float32),
        np.asarray(L0_ffn_up_w, dtype=np.float32),
        np.asarray(L0_ffn_down_w, dtype=np.float32))
    bq = c(b_attn[:H].reshape(32, 128).T)
    bk = c(b_attn[H:2 * H].reshape(32, 128).T)
    bv_row = c(b_attn[2 * H:][None, :])
    bp_row = c(np.asarray(L0_attn_b_proj, dtype=np.float32)[None, :])
    in_maps = []
    for core in range(NCORES):
        m = _prep_core(core, x)
        m.update({"wq": wq_t, "wk": wk_t, "wv": wv, "bq": bq, "bk": bk,
                  "bv_row": bv_row, "wp": wp, "bp_row": bp_row,
                  "gw": gw_t, "uw": uw_t, "dw": dw_t,
                  "gs": np.float32(L0_ffn_gate_scale).reshape(1, 1),
                  "us": np.float32(L0_ffn_up_scale).reshape(1, 1),
                  "ds": np.float32(L0_ffn_down_scale).reshape(1, 1)})
        in_maps.append(m)
    nc = build()
    res = run_bass_kernel_spmd(nc, in_maps, core_ids=list(range(NCORES)))
    y = np.empty((B, S, H), dtype=np.float32)
    for core in range(NCORES):
        b, g = core // 4, core % 4
        sh = res.results[core]["y"]
        y[b, 512 * g:512 * g + 512] = sh[0:512]
        y[b, 512 * (7 - g):512 * (7 - g) + 512] = sh[512:1024]
    return y


# version 08
# speedup vs baseline: 1.0186x; 1.0186x over previous
import numpy as np
import concourse.bass as bass
import concourse.mybir as mybir
from concourse.bass_utils import run_bass_kernel_spmd

F32 = mybir.dt.float32
BF16 = mybir.dt.bfloat16
AF = mybir.ActivationFunctionType
ALU = mybir.AluOpType
AX = mybir.AxisListType

B, S, H = 2, 4096, 4096
NH, D = 32, 128
INTER, IPAD = 11008, 11264
ITILES = IPAD // 128
IQ = ITILES // 4
TOK = 1024
NCORES = 8
S128 = float(1.0 / np.sqrt(128.0))
EPS = 1e-5
NEG = -1.0e9
KB = 1024
NTA, NTB = 20, 32
TBLK = 1024



def _alibi_slopes(n_heads):
    import math
    cp2 = 2 ** math.floor(math.log2(n_heads))
    base = 2 ** (-(2 ** -(math.log2(cp2) - 3)))
    slopes = base ** np.arange(1, cp2 + 1, dtype=np.float64)
    if cp2 != n_heads:
        extra_base = 2 ** (-(2 ** -(math.log2(2 * cp2) - 3)))
        n_rem = min(cp2, n_heads - cp2)
        extra = extra_base ** np.arange(1, 1 + 2 * n_rem, 2, dtype=np.float64)
        slopes = np.concatenate([slopes, extra])
    return slopes.astype(np.float32)


def _perm_for(g):
    rest = [c for c in range(8) if c not in (g, 7 - g)]
    return [g, 7 - g] + rest


def _prep_shared(w_attn, w_proj, gate_w, up_w, down_w):
    c = np.ascontiguousarray
    wq_t = c(w_attn[:, :H].reshape(H, 32, 128).transpose(1, 0, 2))
    wk_t = c(w_attn[:, H:2 * H].reshape(H, 32, 128).transpose(1, 0, 2))
    wv = c(w_attn[:, 2 * H:])
    def itile(w):
        wp = np.zeros((IPAD, H), dtype=np.float32)
        wp[:INTER] = w
        return c(wp.reshape(ITILES, 128, H).transpose(0, 2, 1))
    gw_t, uw_t = itile(gate_w), itile(up_w)
    dwp = np.zeros((IPAD, H), dtype=np.float32)
    dwp[:INTER] = c(down_w.T)
    dw_t = c(dwp.reshape(ITILES, 128, 32, 128).transpose(2, 0, 1, 3))
    return wq_t, wk_t, wv, c(w_proj), gw_t, uw_t, dw_t


def _prep_core(core, x):
    b, g = core // 4, core % 4
    c = np.ascontiguousarray
    order = _perm_for(g)
    perm = np.concatenate([np.arange(512 * ch, 512 * ch + 512) for ch in order])
    xb = c(x[b][perm])
    xown = c(xb[:TOK])
    slopes = _alibi_slopes(NH)
    kpos = perm.astype(np.float32)
    qpos = (kpos[:TOK] - 4096.0)[None, :]
    aA = np.empty((128, NH * NTA), dtype=np.float32)
    aB = np.empty((128, NH * NTB), dtype=np.float32)
    for h in range(NH):
        for t in range(NTA):
            ch = order[t // 4]
            kp = kpos[128 * t:128 * (t + 1)]
            if t >= 4 and ch > g:
                aA[:, h * NTA + t] = NEG
            else:
                aA[:, h * NTA + t] = slopes[h] * (kp - 4096.0)
        for t in range(NTB):
            ch = order[t // 4]
            kp = kpos[128 * t:128 * (t + 1)]
            if (t < 4 or t >= 8) and ch > 7 - g:
                aB[:, h * NTB + t] = NEG
            else:
                aB[:, h * NTB + t] = slopes[h] * (kp - 4096.0)
    j = np.arange(128)
    tri = np.where(j[:, None] > j[None, :], np.float32(NEG), np.float32(0.0))
    mtri = np.tile(tri, (1, 8)).astype(np.float32)
    return {
        "x_b": xb, "x_own": xown, "qpos": c(qpos),
        "alibi_a": aA, "alibi_b": aB, "mtri": mtri,
        "slopes": c(slopes[None, :]),
        "ones_row": np.ones((1, 128), dtype=np.float32),
        "ones_col": np.ones((128, 1), dtype=np.float32),
        "cc": np.tile(np.array([[EPS, -0.5, S128]], dtype=np.float32), (128, 1)),
        "ones128": np.ones((128, 128), dtype=np.float32),
    }



class Sched:
    ENGS = ("sp", "pool", "pe", "ve", "ac")

    def __init__(self):
        self.ops = {e: [] for e in self.ENGS}
        self.cnt = {}
        self.sems = {}

    def sem(self, name):
        self.cnt.setdefault(name, 0)
        return name

    def op(self, eng, fn, waits=(), inc=None, inc_by=1):
        waits = tuple((s, v) for (s, v) in waits if v > 0)
        if inc is not None:
            self.cnt[inc] = self.cnt.get(inc, 0) + inc_by
        target = self.cnt.get(inc, 0) if inc is not None else None
        sems = self.sems

        def run(e):
            for (s, v) in waits:
                e.wait_ge(sems[s], v)
            ins = fn(e)
            if inc is not None:
                ins.then_inc(sems[inc], inc_by)
        self.ops[eng].append(run)
        return target

    def dma(self, q, out, in_, waits=(), **kw):
        sem = self.sem("q_" + q)
        return self.op(q, lambda e: e.dma_start(out=out, in_=in_, **kw),
                       waits=waits, inc=sem, inc_by=16)

    def dmaT(self, q, out, in_, waits=()):
        sem = self.sem("q_" + q)
        return self.op(q, lambda e: e.dma_start_transpose(out, in_),
                       waits=waits, inc=sem, inc_by=16)

    def c(self, name):
        return self.cnt.get(name, 0)

    def barrier(self):
        return [("c_pe", self.c("c_pe")), ("c_ve", self.c("c_ve")),
                ("c_ac", self.c("c_ac")), ("q_sp", self.c("q_sp")),
                ("q_pool", self.c("q_pool"))]


def _dt_size(dtype):
    return {F32: 4, BF16: 2}[dtype]



def build(upto="E", dump=(), dbg=False):
    nc = bass.Bass("TRN2", target_bir_lowering=False)

    def dram_in(name, shape):
        return nc.dram_tensor(name, list(shape), F32, kind="ExternalInput").ap()

    x_b = dram_in("x_b", (S, H))
    x_own = dram_in("x_own", (TOK, H))
    wq_i = dram_in("wq", (32, H, 128))
    wk_i = dram_in("wk", (32, H, 128))
    wv_i = dram_in("wv", (H, H))
    bq_i = dram_in("bq", (128, 32))
    bk_i = dram_in("bk", (128, 32))
    bv_i = dram_in("bv_row", (1, H))
    wp_i = dram_in("wp", (H, H))
    bp_i = dram_in("bp_row", (1, H))
    gw_i = dram_in("gw", (ITILES, H, 128))
    uw_i = dram_in("uw", (ITILES, H, 128))
    dw_i = dram_in("dw", (32, ITILES, 128, 128))
    gs_i = dram_in("gs", (1, 1))
    us_i = dram_in("us", (1, 1))
    ds_i = dram_in("ds", (1, 1))
    slopes_i = dram_in("slopes", (1, NH))
    aA_i = dram_in("alibi_a", (128, NH * NTA))
    aB_i = dram_in("alibi_b", (128, NH * NTB))
    qpos_i = dram_in("qpos", (1, TOK))
    mtri_i = dram_in("mtri", (128, TOK))
    ones_row_i = dram_in("ones_row", (1, 128))
    ones_col_i = dram_in("ones_col", (128, 1))
    cc_i = dram_in("cc", (128, 3))
    ones128_i = dram_in("ones128", (128, 128))

    y_out = nc.dram_tensor("y", [TOK, H], F32, kind="ExternalOutput").ap()

    def scr(name, shape, dtype):
        kind = "ExternalOutput" if name in dump else "Internal"
        return nc.dram_tensor(name, list(shape), dtype, kind=kind).ap()

    xn_dram = scr("xn_dram", (S, H), BF16)
    xn_own_dram = scr("xn_own_dram", (TOK, H), BF16)
    qT_dram = scr("qT_dram", (H, TOK), BF16)
    kT_dram = scr("kT_dram", (H, S), BF16)
    v_dram = scr("v_dram", (NH, S, 128), BF16)
    aoT_dram = scr("aoT_dram", (H, TOK), BF16)
    proj_dram = scr("proj_dram", (TOK, H), BF16)
    dpA = scr("dpA", (H, TOK), BF16)
    dpB = scr("dpB", (H, TOK), BF16)
    sumT_dram = scr("sumT_dram", (H, TOK), BF16)
    dbg_p = scr("dbg_p", (48, 128, 512), F32) if dbg else None
    dbg_s = scr("dbg_s", (48, 128, 512), F32) if dbg else None

    base = (nc.sbuf_base + 31) & ~31
    avail = nc.sbuf_top - base

    def sb(name, off_kb, shape, dtype):
        off = int(off_kb * KB)
        end = off + int(np.prod(shape[1:])) * _dt_size(dtype)
        assert end <= avail, f"sbuf overflow {name}: {end / KB:.1f} > {avail / KB:.1f}"
        return nc.alloc_sbuf_tensor_at(name, list(shape), dtype, offset=base + off)

    bq_sc = sb("bq_sc", 0.0, [128, 32], F32)
    bk_sc = sb("bk_sc", 0.125, [128, 32], F32)
    ones_row = sb("ones_row", 0.25, [1, 128], F32)
    ones_bf = sb("ones_bf", 0.75, [128, 1], BF16)
    slopes_sb = sb("slopes_sb", 0.8125, [1, NH], F32)
    gs_sb = sb("gs_sb", 0.9375, [1, 1], F32)
    us_sb = sb("us_sb", 1.0, [1, 1], F32)
    ds_sb = sb("ds_sb", 1.0625, [1, 1], F32)
    usds_sb = sb("usds_sb", 1.125, [1, 1], F32)
    gs_col = sb("gs_col", 1.1875, [128, 1], F32)
    usds_col = sb("usds_col", 1.25, [128, 1], F32)
    col_buf = sb("col_buf", 1.3125, [128, 8], F32)
    kcm = sb("kcm", 1.375, [1, 16], F32)
    cc_sb = sb("cc_sb", 10.25, [128, 3], F32)
    ones_row_bf = sb("ones_row_bf", 9.8125, [1, 128], BF16)
    ones128 = sb("ones128", 10.5, [128, 128], BF16)
    qpos_sb = sb("qpos_sb", 1.4375, [1, TOK], F32)
    mtri_sb = sb("mtri_sb", 5.4375, [128, TOK], F32)
    bn_sb = sb("bn_sb", 9.4375, [128, 48], F32)
    PERM = 11.0

    xnT_blk = sb("xnT_blk", PERM, [128, 32 * TBLK], BF16)
    A2 = PERM + 64
    wqk_sb = [sb(f"wqk{i}", A2 + 8 * i, [128, 32 * 128], BF16) for i in range(2)]
    wvs_sb = [sb(f"wvs{i}", A2 + 16 + 32 * i, [128, 32 * 512], BF16) for i in range(2)]
    x_tile = [sb(f"x_tile{i}", A2 + 80 + 16 * i, [128, H], F32) for i in range(2)]
    xn_stage = [sb("xn_st0", A2 + 112, [128, H], BF16)]
    qk_stage = [sb(f"qk_st{i}", A2 + 120 + 1 * i, [128, 512], BF16) for i in range(2)]
    v_stage = [sb(f"v_st{i}", A2 + 122 + 1 * i, [128, 512], BF16) for i in range(2)]
    bv_bc = sb("bv_bc", A2 + 124, [128, H], BF16)

    BR = PERM
    qT_sb = sb("qT_sb", BR, [128, TOK], BF16)
    kT_sb = sb("kT_sb", BR + 2, [128, S], BF16)
    v_sb = sb("v_sb", BR + 10, [128, 32 * 128], BF16)
    qsq_sb = sb("qsq_sb", BR + 18, [128, TOK], BF16)
    ksq_sb = sb("ksq_sb", BR + 20, [128, S], BF16)
    hb_sb = sb("hb_sb", BR + 28, [128, TOK], F32)
    hbt_sb = sb("hbt_sb", BR + 32, [128, TOK], F32)
    aA_sb = sb("aA_sb", BR + 36, [128, NH * NTA], F32)
    aB_sb = sb("aB_sb", BR + 38.5, [128, NH * NTB], F32)
    s_buf = sb("s_buf", BR + 43, [128, 3072], F32)
    p_buf = sb("p_buf", BR + 55, [128, 3072], BF16)
    r_sb = sb("r_sb", BR + 61, [128, 512], F32)
    ao_stage = [sb(f"ao_st{i}", BR + 63 + 1 * i, [128, 512], BF16) for i in range(2)]
    qn_row = sb("qn_row", BR + 65, [1, TOK], F32)
    h_row = sb("h_row", BR + 69, [1, TOK], F32)
    h_row2 = sb("h_row2", BR + 73, [1, TOK], F32)

    CR = PERM
    aoT_sb = sb("aoT_sb", CR, [128, 32 * TOK], BF16)
    wpj_sb = [sb(f"wpj{i}", CR + 64 + 32 * i, [128, 32 * 512], BF16) for i in range(2)]
    bp_bc = sb("bp_bc", CR + 128, [128, H], F32)
    pj_stage = [sb(f"pj_st{i}", CR + 144 + 1 * i, [128, 512], BF16) for i in range(2)]

    DR = PERM
    xnT_own = sb("xnT_own", DR, [128, 32 * TOK], BF16)
    inter_sb = sb("inter_sb", DR + 64, [128, IQ * TOK], BF16)
    DS = DR + 108
    gw_sb = [sb(f"gw_sb{i}", DS + 8 * i, [128, 32 * 128], BF16) for i in range(2)]
    uw_sb = [sb(f"uw_sb{i}", DS + 16 + 8 * i, [128, 32 * 128], BF16) for i in range(2)]
    dw_sb = [sb(f"dw_sb{i}", DS + 32 + 5.5 * i, [128, IQ * 128], BF16) for i in range(2)]
    gsil_sb = sb("gsil_sb", DS + 43, [128, TOK], BF16)
    ubf_sb = sb("ubf_sb", DS + 45, [128, TOK], BF16)
    dp_tile = [sb(f"dp_t{i}", DS + 47 + 1 * i, [128, 512], BF16) for i in range(2)]
    d_stage = [sb(f"d_st{i}", DS + 49 + 1 * i, [128, 512], BF16) for i in range(2)]

    ER = PERM
    sumTT = [sb(f"sumTT{i}", ER + 8 * i, [128, H], BF16) for i in range(2)]
    x_e = [sb(f"x_e{i}", ER + 16 + 16 * i, [128, H], F32) for i in range(2)]
    pj_e = [sb(f"pj_e{i}", ER + 48 + 8 * i, [128, H], BF16) for i in range(2)]
    y1_sb = [sb(f"y1_{i}", ER + 64 + 16 * i, [128, H], F32) for i in range(2)]
    y_sb = [sb(f"y_{i}", ER + 96 + 16 * i, [128, H], F32) for i in range(2)]

    PS = [nc.alloc_psum_tensor(f"PS{i}", [128, 1024], F32) for i in range(4)]

    def psA(i):
        return PS[i // 2][:, (i % 2) * 512:(i % 2) * 512 + 512]

    sch = Sched()
    for s in ("q_sp", "q_pool", "c_pe", "c_ve", "c_ac"):
        sch.sem(s)

    for (dst, src) in [(bq_sc, bq_i), (bk_sc, bk_i), (ones_row, ones_row_i),
                       (slopes_sb, slopes_i), (gs_sb, gs_i), (us_sb, us_i),
                       (ds_sb, ds_i), (qpos_sb, qpos_i), (mtri_sb, mtri_i),
                       (cc_sb, cc_i)]:
        sch.dma("sp", dst[:, :], src)
    sch.dma("pool", ones_bf[:, :], ones_col_i)
    sch.dma("pool", ones128[:, :], ones128_i)
    ld0 = sch.c("q_sp")
    sch.op("ve", lambda e: e.tensor_scalar_mul(bq_sc[:, :], bq_sc[:, :], S128),
           waits=[("q_sp", ld0)], inc="c_ve")
    sch.op("ve", lambda e: e.tensor_scalar_mul(bk_sc[:, :], bk_sc[:, :], S128),
           inc="c_ve")
    sch.op("ve", lambda e: e.tensor_tensor(usds_sb[:, :], us_sb[:, :], ds_sb[:, :],
                                           ALU.mult), inc="c_ve")
    ve0 = sch.c("c_ve")
    bvld = sch.dma("pool", bv_bc[0:1, :], bv_i)
    orbld = sch.dma("pool", ones_row_bf[:, :], ones_row_i)
    for j in range(8):
        sch.op("pe", lambda e, j=j: e.matmul(PS[0][:, 0:512], ones_row_bf[:, :],
                                             bv_bc[0:1, j * 512:(j + 1) * 512],
                                             start=True, stop=True),
               waits=([("q_pool", orbld), ("c_ve", ve0), ("q_sp", ld0)] if j == 0
                      else [("c_ac", sch.c("c_ac"))]),
               inc="c_pe")
        g = sch.c("c_pe")
        sch.op("ac", lambda e, j=j: e.copy(bv_bc[:, j * 512:(j + 1) * 512],
                                           PS[0][:, 0:512]),
               waits=[("c_pe", g)], inc="c_ac")
    sch.op("pe", lambda e: e.matmul(PS[1][:, 0:1], ones_row[:, :], gs_sb[:, :],
                                    start=True, stop=True),
           waits=[("q_sp", ld0)], inc="c_pe")
    sch.op("pe", lambda e: e.matmul(PS[1][:, 1:2], ones_row[:, :], usds_sb[:, :],
                                    start=True, stop=True),
           waits=[("c_ve", ve0)], inc="c_pe")
    g = sch.c("c_pe")
    sch.op("ac", lambda e: e.copy(gs_col[:, :], PS[1][:, 0:1]),
           waits=[("c_pe", g)], inc="c_ac")
    sch.op("ac", lambda e: e.copy(usds_col[:, :], PS[1][:, 1:2]), inc="c_ac")
    ac0 = sch.c("c_ac")

    lastLN = {"aff": [0, 0], "st": 0}

    def ln_tile(src_ap, dst_rows_ap, par):
        xt, xs = x_tile[par], xn_stage[0]
        ld = sch.dma("sp", xt[:, :], src_ap,
                     waits=[("c_ve", lastLN["aff"][par])])
        for j in range(8):
            sch.op("ve", lambda e, j=j: e.bn_stats(bn_sb[:, j * 6:j * 6 + 6],
                                                   xt[:, j * 512:(j + 1) * 512]),
                   waits=([("q_sp", ld)] if j == 0 else []),
                   inc=("c_ve" if j == 7 else None))
        vbn = sch.c("c_ve")
        sch.op("ve", lambda e: e.bn_aggr(col_buf[:, 0:2], bn_sb[:, :]),
               waits=[("c_ve", vbn)], inc="c_ve")
        vagg = sch.c("c_ve")
        sch.op("ac", lambda e: e.activation(col_buf[:, 2:3], col_buf[:, 1:2],
                                            AF.Ln, bias=cc_sb[:, 0:1]),
               waits=[("c_ve", vagg)], inc="c_ac")
        a1 = sch.c("c_ac")
        sch.op("ac", lambda e: e.activation(col_buf[:, 3:4], col_buf[:, 2:3],
                                            AF.Exp, scale=cc_sb[:, 1:2]),
               waits=[("c_ac", a1)], inc="c_ac")
        a2 = sch.c("c_ac")
        sch.op("ve", lambda e: e.tensor_scalar(col_buf[:, 4:5], col_buf[:, 0:1],
                                               col_buf[:, 3:4], -1.0,
                                               ALU.mult, ALU.mult),
               waits=[("c_ac", a2)], inc="c_ve")
        vB = sch.c("c_ve")
        sch.op("ve", lambda e: e.tensor_scalar(xs[:, :], xt[:, :],
                                               col_buf[:, 3:4], col_buf[:, 4:5],
                                               ALU.mult, ALU.add),
               waits=[("c_ve", vB), ("q_sp", lastLN["st"])], inc="c_ve")
        aff = sch.c("c_ve")
        st = sch.dma("sp", dst_rows_ap, xs[:, :], waits=[("c_ve", aff)])
        lastLN["aff"][par], lastLN["st"] = aff, st
        return st

    st_free = {"qk": [0, 0], "v": [0, 0]}
    ps_free_qk = [ac0, ac0]
    ps_free_v = [ac0, ac0]
    qkstrip_free = [0, 0]
    vstrip_free = [0, 0]
    xnT_free = [0]
    qk_pp = [0]
    qk_sp = [0]

    def qk_part(w_in, bias_sb, dst_dram, dst_col0, n_tcols, ct, tr_last):
        par = ct % 2
        ld = sch.dma("pool", wqk_sb[par][:, :].rearrange("p (g c) -> p g c", g=32),
                     w_in.rearrange("(g p) c -> p g c", p=128),
                     waits=[("c_pe", qkstrip_free[par])])
        for th in range(n_tcols):
            pp = qk_pp[0]
            qk_pp[0] ^= 1
            spar = qk_sp[0]
            qk_sp[0] = (qk_sp[0] + 1) % 2
            wmm = [("q_pool", ld), ("q_sp", tr_last), ("c_ve", ps_free_qk[pp])]
            for hcc in range(32):
                sch.op("pe", lambda e, hcc=hcc, pp=pp, par=par, th=th:
                       e.matmul(psA(pp),
                                wqk_sb[par][:, hcc * 128:(hcc + 1) * 128],
                                xnT_blk[:, hcc * TBLK + th * 512:
                                        hcc * TBLK + th * 512 + 512],
                                start=(hcc == 0), stop=(hcc == 31)),
                       waits=(wmm if hcc == 0 else []),
                       inc=("c_pe" if hcc == 31 else None))
            gpe = sch.c("c_pe")
            sch.op("ve", lambda e, pp=pp, spar=spar, ct=ct:
                   e.tensor_scalar(qk_stage[spar][:, :], psA(pp),
                                   S128, bias_sb[:, ct:ct + 1],
                                   ALU.mult, ALU.add),
                   waits=[("c_pe", gpe), ("q_sp", st_free["qk"][spar])],
                   inc="c_ve")
            ps_free_qk[pp] = sch.c("c_ve")
            aev = sch.c("c_ve")
            st_free["qk"][spar] = sch.dma(
                "sp", dst_dram[ct * 128:(ct + 1) * 128,
                               dst_col0 + th * 512:dst_col0 + th * 512 + 512],
                qk_stage[spar][:, :], waits=[("c_ve", aev)])
        qkstrip_free[par] = sch.c("c_pe")

    def ln_block(tb):
        st_last = 0
        for tt in range(8):
            rows = tb * TBLK + tt * 128
            st_last = ln_tile(x_b[rows:rows + 128, :],
                              xn_dram[rows:rows + 128, :], tt % 2)
        return st_last

    def phaseA():
        ln_st = [0] * 5
        for tb in range(4):
            if tb == 0:
                ln_st[0] = ln_block(0)
            st_last = ln_st[tb]
            tr_last = 0
            for hc in range(32):
                w = []
                if hc == 0:
                    w = [("q_sp", st_last)]
                    if xnT_free[0]:
                        w.append(("c_pe", xnT_free[0]))
                tr_last = sch.dmaT("sp", xnT_blk[:, hc * TBLK:(hc + 1) * TBLK],
                                   xn_dram[tb * TBLK:(tb + 1) * TBLK,
                                           hc * 128:(hc + 1) * 128], waits=w)
            for ct in range(32):
                qk_part(wk_i[ct], bk_sc, kT_dram, tb * TBLK, TBLK // 512, ct,
                        tr_last)
                if ct % 4 == 0:
                    tt = ct // 4
                    nb = tb + 1
                    if nb < 4:
                        rows = nb * TBLK + tt * 128
                        ln_st[nb] = ln_tile(x_b[rows:rows + 128, :],
                                            xn_dram[rows:rows + 128, :], tt % 2)
                    else:
                        rows = tt * 128
                        ln_st[nb] = ln_tile(x_own[rows:rows + 128, :],
                                            xn_own_dram[rows:rows + 128, :],
                                            tt % 2)
            if tb == 0:
                for ct in range(32):
                    qk_part(wq_i[ct], bq_sc, qT_dram, 0, TOK // 512, ct, tr_last)
            for vc in range(8):
                par = vc % 2
                ldv = sch.dma("pool",
                              wvs_sb[par][:, :].rearrange("p (g c) -> p g c", g=32),
                              wv_i[:, vc * 512:(vc + 1) * 512].rearrange(
                                  "(g p) c -> p g c", p=128),
                              waits=[("c_pe", vstrip_free[par])])
                for tt in range(TBLK // 128):
                    pp = 2 + (tt % 2)
                    spar = tt % 2
                    wmm = [("q_pool", ldv), ("c_ve", ps_free_v[pp - 2]),
                           ("c_ac", ac0)]
                    for hcc in range(32):
                        sch.op("pe", lambda e, hcc=hcc, pp=pp, par=par, tt=tt:
                               e.matmul(psA(pp),
                                        xnT_blk[:, hcc * TBLK + tt * 128:
                                                hcc * TBLK + tt * 128 + 128],
                                        wvs_sb[par][:, hcc * 512:(hcc + 1) * 512],
                                        start=(hcc == 0), stop=(hcc == 31)),
                               waits=(wmm if hcc == 0 else []),
                               inc=("c_pe" if hcc == 31 else None))
                    gpe = sch.c("c_pe")
                    sch.op("ve", lambda e, pp=pp, spar=spar, vc=vc:
                           e.tensor_tensor(v_stage[spar][:, :], psA(pp),
                                           bv_bc[:, vc * 512:(vc + 1) * 512],
                                           ALU.add),
                           waits=[("c_pe", gpe), ("q_sp", st_free["v"][spar])],
                           inc="c_ve")
                    ps_free_v[pp - 2] = sch.c("c_ve")
                    vev = sch.c("c_ve")
                    rows = tb * TBLK + tt * 128
                    last = 0
                    for i in range(4):
                        h = vc * 4 + i
                        last = sch.dma("sp", v_dram[h, rows:rows + 128, :],
                                       v_stage[spar][:, i * 128:(i + 1) * 128],
                                       waits=([("c_ve", vev)] if i == 0 else []))
                    st_free["v"][spar] = last
                vstrip_free[par] = sch.c("c_pe")
            xnT_free[0] = sch.c("c_pe")

    def phaseB():
        bar = sch.barrier()
        la = sch.dma("sp", aA_sb[:, :], aA_i, waits=bar)
        lb = sch.dma("sp", aB_sb[:, :], aB_i)
        p_free = [0, 0, 0]
        s_free = {0: 0, 1: 0, 2: 0}
        ps_free = [0, 0]
        bcast_ac = [0]
        av_free = [0, 0]
        ao_free = [0, 0]
        prev_pe = [0]
        norm_ve = [0, 0]
        nslot = [PS[3][:, 0:512], PS[3][:, 512:1024]]

        for h in range(NH):
            wq_ = [("c_pe", prev_pe[0])] if prev_pe[0] else [("q_sp", lb)]
            lq = sch.dma("sp", qT_sb[:, :], qT_dram[h * 128:(h + 1) * 128, :],
                         waits=wq_)
            lk = sch.dma("sp", kT_sb[:, :], kT_dram[h * 128:(h + 1) * 128, :])
            lv = sch.dma("sp", v_sb[:, :].rearrange("p (g d) -> p g d", g=32),
                         v_dram[h].rearrange("(g p) d -> p g d", p=128))
            sch.op("ac", lambda e: e.activation(qsq_sb[:, :], qT_sb[:, :],
                                                AF.Square),
                   waits=[("q_sp", lq)], inc="c_ac")
            aq = sch.c("c_ac")
            sch.op("ac", lambda e: e.activation(ksq_sb[:, :], kT_sb[:, :],
                                                AF.Square),
                   waits=[("q_sp", lk)], inc="c_ac")
            ak = sch.c("c_ac")
            for j in range(2):
                sl = j % 2
                sch.op("pe", lambda e, j=j, sl=sl: e.matmul(
                    nslot[sl], ones128[:, :], qsq_sb[:, j * 512:(j + 1) * 512],
                    start=True, stop=True),
                    waits=[("c_ac", aq), ("c_ve", norm_ve[sl])], inc="c_pe")
                g = sch.c("c_pe")
                sch.op("ve", lambda e, j=j, sl=sl: e.tensor_copy(
                    qn_row[0:1, j * 512:(j + 1) * 512], nslot[sl][0:1, :]),
                    waits=[("c_pe", g)], inc="c_ve")
                norm_ve[sl] = sch.c("c_ve")
            for j in range(8):
                sl = j % 2
                sch.op("pe", lambda e, j=j, sl=sl: e.matmul(
                    nslot[sl], ones128[:, :], ksq_sb[:, j * 512:(j + 1) * 512],
                    start=True, stop=True),
                    waits=[("c_ac", ak), ("c_ve", norm_ve[sl])], inc="c_pe")
                g = sch.c("c_pe")
                sch.op("ve", lambda e, j=j, sl=sl: e.reduce_max(
                    kcm[0:1, j:j + 1], nslot[sl][0:1, :], axis=AX.X),
                    waits=[("c_pe", g)], inc="c_ve")
                norm_ve[sl] = sch.c("c_ve")
            sch.op("ve", lambda e: e.reduce_max(kcm[0:1, 8:9], kcm[0:1, 0:8],
                                                axis=AX.X),
                   waits=[("c_ve", max(norm_ve))], inc="c_ve")
            v1 = sch.c("c_ve")
            sch.op("ve", lambda e: e.tensor_scalar_mul(kcm[0:1, 9:10],
                                                       kcm[0:1, 8:9], -0.5),
                   waits=[("c_ve", v1)], inc="c_ve")
            v2 = sch.c("c_ve")
            sch.op("ve", lambda e: e.tensor_scalar(h_row[0:1, :], qn_row[0:1, :],
                                                   -0.5, kcm[0:1, 9:10],
                                                   ALU.mult, ALU.add),
                   waits=[("c_ve", v2)], inc="c_ve")
            v3 = sch.c("c_ve")
            hh = h
            sch.op("ve", lambda e, hh=hh: e.tensor_scalar_mul(
                kcm[0:1, 10:11], slopes_sb[0:1, hh:hh + 1], -1.0),
                waits=[("c_ve", v3)], inc="c_ve")
            v4 = sch.c("c_ve")
            sch.op("ve", lambda e: e.scalar_tensor_tensor(
                h_row2[0:1, :], qpos_sb[0:1, :], kcm[0:1, 10:11], h_row[0:1, :],
                ALU.mult, ALU.add),
                waits=[("c_ve", v4)], inc="c_ve")
            v5 = sch.c("c_ve")
            norm_ve[0] = max(norm_ve[0], v5)
            for j in range(2):
                sch.op("pe", lambda e, j=j: e.matmul(
                    PS[3][:, 512:1024], ones_row[:, :],
                    h_row2[0:1, j * 512:(j + 1) * 512], start=True, stop=True),
                    waits=[("c_ve", v5), ("c_ac", sch.c("c_ac"))], inc="c_pe")
                g = sch.c("c_pe")
                sch.op("ac", lambda e, j=j: e.copy(
                    hb_sb[:, j * 512:(j + 1) * 512], PS[3][:, 512:1024]),
                    waits=[("c_pe", g)], inc="c_ac")
            ahb = sch.c("c_ac")
            sch.op("ve", lambda e: e.tensor_tensor(hbt_sb[:, :], hb_sb[:, :],
                                                   mtri_sb[:, :], ALU.add),
                   waits=[("c_ac", ahb)], inc="c_ve")
            vhbt = sch.c("c_ve")
            bcast_ac[0] = ahb

            gidx = [0]

            def chunk_loop(q0, ntiles, diag0, ab_sb, ab_base, avp):
                SL = [PS[0], PS[2]]
                npairs = ntiles // 2
                dp0 = diag0 // 2
                porder = ([u for u in range(npairs) if not dp0 <= u < dp0 + 2]
                          + [dp0, dp0 + 1])
                pending = []
                s_free_at = {}
                memset_at = {}
                nunits = len(porder)

                def emit_denav(slot_u, u):
                    for half in range(2):
                        g2 = 2 * u + half
                        seq = den_seq[0]
                        den_seq[0] += 1
                        pv = p_buf[:, ((2 * slot_u + half) % 6) * 512:
                                   ((2 * slot_u + half) % 6) * 512 + 512]
                        wden = [("c_ac", s_free_at[g2])]
                        if g2 in memset_at:
                            wden.append(("c_ve", memset_at[g2]))
                        if seq == 0:
                            wden.append(("c_ve", av_free[avp]))
                            wden.append(("c_ve", max(norm_ve)))
                            wden.append(("c_ac", bcast_ac[0]))
                        sch.op("pe", lambda e, pv=pv, seq=seq:
                               e.matmul(PS[3][:, avp * 512:avp * 512 + 512],
                                        ones128[:, :], pv,
                                        start=(seq == 0), stop=(seq == ntiles - 1)),
                               waits=wden)
                        sch.op("pe", lambda e, pv=pv, g2=g2, seq=seq:
                               e.matmul(psA(2 + avp),
                                        v_sb[:, g2 * 128:(g2 + 1) * 128], pv,
                                        start=(seq == 0), stop=(seq == ntiles - 1)),
                               inc=("c_pe" if half == 1 else None))
                    p_free[slot_u % 3] = sch.c("c_pe")

                den_seq = [0]
                for idx, u in enumerate(porder):
                    su = gidx[0] % 2
                    slot_u = gidx[0] % 3
                    gidx[0] += 1
                    g0, g1 = 2 * u, 2 * u + 1
                    diag = dp0 <= u < dp0 + 2
                    for half, g_ in ((0, g0), (1, g1)):
                        m = g_ - diag0
                        lo = 128 * m if (diag and m > 0) else 0
                        wsc = []
                        if half == 0:
                            wsc = [("c_ve", ps_free[su])]
                            if idx == 0:
                                wsc += [("c_ve", vhbt), ("q_sp", lv), ("q_sp", la)]
                        sch.op("pe", lambda e, g_=g_, su=su, q0=q0, lo=lo, half=half:
                               e.matmul(SL[su][:, half * 512 + lo:half * 512 + 512],
                                        kT_sb[:, g_ * 128:(g_ + 1) * 128],
                                        qT_sb[:, q0 + lo:q0 + 512],
                                        start=True, stop=True),
                               waits=wsc,
                               inc=("c_pe" if half == 1 else None))
                    gpe = sch.c("c_pe")
                    if len(pending) >= 2:
                        emit_denav(*pending.pop(0))
                    wv_ = [("c_pe", gpe), ("c_ac", s_free.get(slot_u, 0))]
                    if not diag:
                        hb2 = hb_sb[:, q0:q0 + 512].rearrange(
                            "p (o q) -> p o q", o=1).broadcast_to([128, 2, 512])
                        sch.op("ve", lambda e, su=su, slot_u=slot_u, hb2=hb2:
                               e.tensor_tensor(
                                   s_buf[:, slot_u * 1024:slot_u * 1024 + 1024]
                                   .rearrange("p (o q) -> p o q", o=2),
                                   SL[su][:, :].rearrange("p (o q) -> p o q", o=2),
                                   hb2, ALU.add),
                               waits=wv_, inc="c_ve")
                    else:
                        for half, g_ in ((0, g0), (1, g1)):
                            m = g_ - diag0
                            lo = 128 * m
                            sch.op("ve", lambda e, su=su, slot_u=slot_u, q0=q0,
                                   lo=lo, half=half:
                                   e.tensor_tensor(
                                       s_buf[:, slot_u * 1024 + half * 512 + lo:
                                             slot_u * 1024 + half * 512 + lo + 128],
                                       SL[su][:, half * 512 + lo:half * 512 + lo + 128],
                                       hbt_sb[:, q0 + lo:q0 + lo + 128], ALU.add),
                                   waits=(wv_ if half == 0 else []),
                                   inc=("c_ve" if (half == 1 and m == 3) else None))
                            if m < 3:
                                sch.op("ve", lambda e, su=su, slot_u=slot_u, q0=q0,
                                       lo=lo, half=half:
                                       e.tensor_tensor(
                                           s_buf[:, slot_u * 1024 + half * 512 + lo + 128:
                                                 slot_u * 1024 + half * 512 + 512],
                                           SL[su][:, half * 512 + lo + 128:
                                                  half * 512 + 512],
                                           hb_sb[:, q0 + lo + 128:q0 + 512], ALU.add),
                                       inc=("c_ve" if half == 1 else None))
                    gve = sch.c("c_ve")
                    ps_free[su] = gve
                    for half, g_ in ((0, g0), (1, g1)):
                        m = g_ - diag0
                        lo = 128 * m if (diag and m > 0) else 0
                        pslot = (2 * slot_u + half) % 6
                        if diag and m > 0:
                            sch.op("ve", lambda e, pslot=pslot, m=m:
                                   e.memset(p_buf[:, pslot * 512:pslot * 512 + 128 * m],
                                            0.0),
                                   waits=[("c_pe", p_free[slot_u % 3])],
                                   inc="c_ve")
                            memset_at[g_] = sch.c("c_ve")
                        col = ab_base + g_
                        sch.op("ac", lambda e, pslot=pslot, slot_u=slot_u, half=half,
                               lo=lo, col=col:
                               e.activation(p_buf[:, pslot * 512 + lo:pslot * 512 + 512],
                                            s_buf[:, slot_u * 1024 + half * 512 + lo:
                                                  slot_u * 1024 + half * 512 + 512],
                                            AF.Exp, bias=ab_sb[:, col:col + 1]),
                               waits=[("c_ve", gve),
                                      ("c_pe", p_free[slot_u % 3])], inc="c_ac")
                        s_free_at[g_] = sch.c("c_ac")
                    s_free[slot_u] = sch.c("c_ac")
                    pending.append((slot_u, u))
                while pending:
                    emit_denav(*pending.pop(0))
                gend = sch.c("c_pe")
                sch.op("ve", lambda e, avp=avp: e.reciprocal(
                    r_sb[:, :], PS[3][:, avp * 512:avp * 512 + 512]),
                    waits=[("c_pe", gend)], inc="c_ve")
                norm_ve[avp] = max(norm_ve[avp], sch.c("c_ve"))
                aop = avp
                sch.op("ve", lambda e, avp=avp, aop=aop: e.tensor_tensor(
                    ao_stage[aop][:, :], psA(2 + avp), r_sb[:, :], ALU.mult),
                    waits=[("q_sp", ao_free[aop])], inc="c_ve")
                av_free[avp] = sch.c("c_ve")
                ao_free[aop] = sch.dma(
                    "sp", aoT_dram[h * 128:(h + 1) * 128, q0:q0 + 512],
                    ao_stage[aop][:, :], waits=[("c_ve", av_free[avp])])

            chunk_loop(0, NTA, 0, aA_sb, h * NTA, 0)
            chunk_loop(512, NTB, 4, aB_sb, h * NTB, 1)
            prev_pe[0] = sch.c("c_pe")

    def phaseC():
        bar = sch.barrier()
        lao = sch.dma("sp", aoT_sb[:, :].rearrange("p (g t) -> p g t", g=32),
                      aoT_dram.rearrange("(g p) t -> p g t", p=128), waits=bar)
        lbp = sch.dma("sp", bp_bc[0:1, :], bp_i)
        for j in range(8):
            sch.op("pe", lambda e, j=j: e.matmul(
                PS[0][:, 0:512], ones_row[:, :],
                bp_bc[0:1, j * 512:(j + 1) * 512], start=True, stop=True),
                waits=([("q_sp", lbp)] + bar[:3] if j == 0
                       else [("c_ac", sch.c("c_ac"))]),
                inc="c_pe")
            g = sch.c("c_pe")
            sch.op("ac", lambda e, j=j: e.copy(bp_bc[:, j * 512:(j + 1) * 512],
                                               PS[0][:, 0:512]),
                   waits=[("c_pe", g)], inc="c_ac")
        acb = sch.c("c_ac")
        pj_pe = [0, 0]
        pj_ve = [0, 0]
        pj_st = [0, 0]
        for cb in range(8):
            par = cb % 2
            wld = [("c_pe", pj_pe[par])] if pj_pe[par] else list(bar)
            ld = sch.dma("pool",
                         wpj_sb[par][:, :].rearrange("p (g c) -> p g c", g=32),
                         wp_i[:, cb * 512:(cb + 1) * 512].rearrange(
                             "(g p) c -> p g c", p=128), waits=wld)
            for t1 in range(8):
                pp = (cb * 8 + t1) % 2
                spar = (cb * 8 + t1) % 2
                wmm = [("q_pool", ld), ("q_sp", lao), ("c_ve", pj_ve[pp]),
                       ("c_ac", acb)]
                for hcc in range(32):
                    sch.op("pe", lambda e, hcc=hcc, pp=pp, par=par, t1=t1:
                           e.matmul(psA(pp),
                                    aoT_sb[:, hcc * TOK + t1 * 128:
                                           hcc * TOK + t1 * 128 + 128],
                                    wpj_sb[par][:, hcc * 512:(hcc + 1) * 512],
                                    start=(hcc == 0), stop=(hcc == 31)),
                           waits=(wmm if hcc == 0 else []),
                           inc=("c_pe" if hcc == 31 else None))
                pj_pe[par] = sch.c("c_pe")
                g = sch.c("c_pe")
                sch.op("ve", lambda e, pp=pp, spar=spar, cb=cb:
                       e.tensor_tensor(pj_stage[spar][:, :], psA(pp),
                                       bp_bc[:, cb * 512:(cb + 1) * 512],
                                       ALU.add),
                       waits=[("c_pe", g), ("q_sp", pj_st[spar])], inc="c_ve")
                pj_ve[pp] = sch.c("c_ve")
                vv = sch.c("c_ve")
                pj_st[spar] = sch.dma(
                    "sp", proj_dram[t1 * 128:(t1 + 1) * 128,
                                    cb * 512:(cb + 1) * 512],
                    pj_stage[spar][:, :], waits=[("c_ve", vv)])

    gu_pe = [0, 0]
    ga_ac = [0, 0]
    ga_ve = [0, 0]
    ga_tt = [0]
    int_free = [0]
    dw_pe = [0, 0]
    pd_evac = [0, 0]
    dst_free = [0, 0]
    dp_ld_free = [0, 0]

    def phaseD_xn(bar):
        tr = 0
        for hc in range(32):
            tr = sch.dmaT("sp", xnT_own[:, hc * TOK:(hc + 1) * TOK],
                          xn_own_dram[:, hc * 128:(hc + 1) * 128],
                          waits=(bar if hc == 0 else []))
        return tr

    def phaseDa(q, trx, mm_extra=()):
        for it in range(IQ):
            gt = q * IQ + it
            par = it % 2
            wld = [("c_pe", gu_pe[par])] if gu_pe[par] else []
            if it < 2:
                wld += list(mm_extra)
            lg = sch.dma("pool",
                         gw_sb[par][:, :].rearrange("p (g c) -> p g c", g=32),
                         gw_i[gt].rearrange("(g p) c -> p g c", p=128),
                         waits=wld)
            lu = sch.dma("pool",
                         uw_sb[par][:, :].rearrange("p (g c) -> p g c", g=32),
                         uw_i[gt].rearrange("(g p) c -> p g c", p=128))
            ppg = 0 if par == 0 else 2
            ppu = 1 if par == 0 else 3
            wmm = [("q_pool", lu), ("q_sp", trx),
                   ("c_ac", ga_ac[par]), ("c_ve", ga_ve[par])]
            if it < 2:
                wmm += list(mm_extra)
            first = [True]
            for th in range(2):
                for hcc in range(32):
                    w = wmm if first[0] else []
                    first[0] = False
                    sch.op("pe", lambda e, hcc=hcc, ppg=ppg, par=par, th=th:
                           e.matmul(PS[ppg][:, th * 512:th * 512 + 512],
                                    gw_sb[par][:, hcc * 128:(hcc + 1) * 128],
                                    xnT_own[:, hcc * TOK + th * 512:
                                            hcc * TOK + th * 512 + 512],
                                    start=(hcc == 0), stop=(hcc == 31)),
                           waits=w)
            for th in range(2):
                for hcc in range(32):
                    sch.op("pe", lambda e, hcc=hcc, ppu=ppu, par=par, th=th:
                           e.matmul(PS[ppu][:, th * 512:th * 512 + 512],
                                    uw_sb[par][:, hcc * 128:(hcc + 1) * 128],
                                    xnT_own[:, hcc * TOK + th * 512:
                                            hcc * TOK + th * 512 + 512],
                                    start=(hcc == 0), stop=(hcc == 31)),
                           waits=[],
                           inc=("c_pe" if (th == 1 and hcc == 31) else None))
            gu_pe[par] = sch.c("c_pe")
            g = sch.c("c_pe")
            sch.op("ac", lambda e, ppg=ppg: e.activation(
                gsil_sb[:, :], PS[ppg][:, :], AF.Silu, scale=gs_col[:, 0:1]),
                waits=[("c_pe", g), ("c_ve", ga_tt[0])], inc="c_ac")
            ga_ac[par] = sch.c("c_ac")
            a1 = sch.c("c_ac")
            sch.op("ve", lambda e, ppu=ppu: e.tensor_scalar_mul(
                ubf_sb[:, :], PS[ppu][:, :], usds_col[:, 0:1]),
                waits=[("c_pe", g)], inc="c_ve")
            u1 = sch.c("c_ve")
            sch.op("ve", lambda e, it=it: e.tensor_tensor(
                inter_sb[:, it * TOK:(it + 1) * TOK], gsil_sb[:, :],
                ubf_sb[:, :], ALU.mult),
                waits=[("c_ve", u1), ("c_ac", a1), ("c_pe", int_free[0])],
                inc="c_ve")
            ga_ve[par] = sch.c("c_ve")
            ga_tt[0] = sch.c("c_ve")

    def phaseDb(q, src_dp, dst_dp, mm_extra=()):
        inter_ready = sch.c("c_ve")
        db_qsp = sch.c("q_sp")
        for ht in range(32):
            par = ht % 2
            wld = [("c_pe", dw_pe[par])] if dw_pe[par] else []
            ld = sch.dma("pool",
                         dw_sb[par][:, :].rearrange("p (it hh) -> p it hh", it=IQ),
                         dw_i[ht, q * IQ:(q + 1) * IQ].rearrange(
                             "it p hh -> p it hh"),
                         waits=wld)
            for th in range(2):
                pp = (ht * 2 + th) % 2
                spar = (ht * 2 + th) % 2
                ldp = 0
                if src_dp is not None:
                    w = [("c_ve", dp_ld_free[spar])]
                    if ht == 0 and th == 0:
                        w.append(("q_sp", db_qsp))
                    ldp = sch.dma("sp", dp_tile[spar][:, :],
                                  src_dp[ht * 128:(ht + 1) * 128,
                                         th * 512:th * 512 + 512], waits=w)
                wmm = [("q_pool", ld), ("c_ve", pd_evac[pp]),
                       ("c_ve", inter_ready)]
                if ht == 0 and th == 0:
                    wmm += list(mm_extra)
                for it in range(IQ):
                    sch.op("pe", lambda e, it=it, pp=pp, par=par, th=th:
                           e.matmul(psA(pp),
                                    dw_sb[par][:, it * 128:(it + 1) * 128],
                                    inter_sb[:, it * TOK + th * 512:
                                             it * TOK + th * 512 + 512],
                                    start=(it == 0), stop=(it == IQ - 1)),
                           waits=(wmm if it == 0 else []),
                           inc=("c_pe" if it == IQ - 1 else None))
                dw_pe[par] = sch.c("c_pe")
                g = sch.c("c_pe")
                rows = slice(ht * 128, (ht + 1) * 128)
                cols = slice(th * 512, th * 512 + 512)
                dd = dst_dp if dst_dp is not None else sumT_dram
                if src_dp is None:
                    sch.op("ve", lambda e, pp=pp, spar=spar: e.tensor_copy(
                        d_stage[spar][:, :], psA(pp)),
                        waits=[("c_pe", g), ("q_sp", dst_free[spar])],
                        inc="c_ve")
                else:
                    sch.op("ve", lambda e, pp=pp, spar=spar:
                           e.scalar_tensor_tensor(
                               d_stage[spar][:, :], psA(pp), 1.0,
                               dp_tile[spar][:, :], ALU.mult, ALU.add),
                           waits=[("c_pe", g), ("q_sp", ldp),
                                  ("q_sp", dst_free[spar])], inc="c_ve")
                    dp_ld_free[spar] = sch.c("c_ve")
                pd_evac[pp] = sch.c("c_ve")
                dst_free[spar] = sch.dma("sp", dd[rows, cols],
                                         d_stage[spar][:, :],
                                         waits=[("c_ve", pd_evac[pp])])

    def phaseE():
        bar = sch.barrier()
        e_ve = [0, 0]
        e_st = [0, 0]
        for tt in range(8):
            par = tt % 2
            w0 = list(bar) if tt == 0 else []
            ltr = sch.dmaT("sp", sumTT[par][:, :],
                           sumT_dram[:, tt * 128:(tt + 1) * 128],
                           waits=w0 + ([("c_ve", e_ve[par])] if e_ve[par] else []))
            lx = sch.dma("sp", x_e[par][:, :], x_own[tt * 128:(tt + 1) * 128, :])
            lp = sch.dma("sp", pj_e[par][:, :],
                         proj_dram[tt * 128:(tt + 1) * 128, :])
            sch.op("ve", lambda e, par=par: e.tensor_tensor(
                y1_sb[par][:, :], x_e[par][:, :], pj_e[par][:, :], ALU.add),
                waits=[("q_sp", lp), ("c_ve", e_ve[par])], inc="c_ve")
            v1 = sch.c("c_ve")
            sch.op("ve", lambda e, par=par: e.tensor_tensor(
                y_sb[par][:, :], y1_sb[par][:, :], sumTT[par][:, :], ALU.add),
                waits=[("c_ve", v1), ("q_sp", e_st[par])], inc="c_ve")
            e_ve[par] = sch.c("c_ve")
            e_st[par] = sch.dma("sp", y_out[tt * 128:(tt + 1) * 128, :],
                                y_sb[par][:, :], waits=[("c_ve", e_ve[par])])
        total = sch.c("q_sp")
        sch.op("sp", lambda e: e.nop(), waits=[("q_sp", total)])

    order = ["0", "A", "Aown", "B", "C", "D", "E"]
    stop = order.index(upto)
    if stop >= 1:
        phaseA()
    if stop >= 3:
        phaseB()
    if stop >= 4:
        phaseC()
    if stop >= 5:
        cbar = [("c_pe", sch.c("c_pe")), ("c_ac", sch.c("c_ac")),
                ("c_ve", sch.c("c_ve"))]
        trx = phaseD_xn(sch.barrier())
        phaseDa(0, trx, mm_extra=cbar)
        phaseDb(0, None, dpA, mm_extra=cbar)
        for q, (s_, d_) in enumerate([(dpA, dpB), (dpB, dpA), (dpA, None)],
                                     start=1):
            int_free[0] = sch.c("c_pe")
            phaseDa(q, trx,
                    mm_extra=[("c_ve", pd_evac[0]), ("c_ve", pd_evac[1])])
            phaseDb(q, s_, d_)
    if stop >= 6:
        phaseE()
    else:
        sch.op("sp", lambda e: e.nop(), waits=[("q_sp", sch.c("q_sp"))])
        sch.op("pool", lambda e: e.nop(), waits=[("q_pool", sch.c("q_pool"))])

    import contextlib
    with contextlib.ExitStack() as stack:
        for name in list(sch.cnt):
            sch.sems[name] = stack.enter_context(nc.semaphore(name))
        with nc.Block() as block:
            @block.sync
            def _(e):
                for f in sch.ops["sp"]:
                    f(e)

            @block.gpsimd
            def _(e):
                for f in sch.ops["pool"]:
                    f(e)

            @block.tensor
            def _(e):
                for f in sch.ops["pe"]:
                    f(e)

            @block.vector
            def _(e):
                for f in sch.ops["ve"]:
                    f(e)

            @block.scalar
            def _(e):
                for f in sch.ops["ac"]:
                    f(e)

    return nc



def kernel(L0_attn_hidden_states, L0_attn_w_attn, L0_attn_b_attn,
           L0_attn_w_proj, L0_attn_b_proj, L0_ffn_gate_w, L0_ffn_up_w,
           L0_ffn_down_w, L0_ffn_gate_scale, L0_ffn_up_scale,
           L0_ffn_down_scale):
    c = np.ascontiguousarray
    x = np.asarray(L0_attn_hidden_states, dtype=np.float32)
    w_attn = np.asarray(L0_attn_w_attn, dtype=np.float32)
    b_attn = np.asarray(L0_attn_b_attn, dtype=np.float32)
    wq_t, wk_t, wv, wp, gw_t, uw_t, dw_t = _prep_shared(
        w_attn, np.asarray(L0_attn_w_proj, dtype=np.float32),
        np.asarray(L0_ffn_gate_w, dtype=np.float32),
        np.asarray(L0_ffn_up_w, dtype=np.float32),
        np.asarray(L0_ffn_down_w, dtype=np.float32))
    bq = c(b_attn[:H].reshape(32, 128).T)
    bk = c(b_attn[H:2 * H].reshape(32, 128).T)
    bv_row = c(b_attn[2 * H:][None, :])
    bp_row = c(np.asarray(L0_attn_b_proj, dtype=np.float32)[None, :])
    in_maps = []
    for core in range(NCORES):
        m = _prep_core(core, x)
        m.update({"wq": wq_t, "wk": wk_t, "wv": wv, "bq": bq, "bk": bk,
                  "bv_row": bv_row, "wp": wp, "bp_row": bp_row,
                  "gw": gw_t, "uw": uw_t, "dw": dw_t,
                  "gs": np.float32(L0_ffn_gate_scale).reshape(1, 1),
                  "us": np.float32(L0_ffn_up_scale).reshape(1, 1),
                  "ds": np.float32(L0_ffn_down_scale).reshape(1, 1)})
        in_maps.append(m)
    nc = build()
    res = run_bass_kernel_spmd(nc, in_maps, core_ids=list(range(NCORES)))
    y = np.empty((B, S, H), dtype=np.float32)
    for core in range(NCORES):
        b, g = core // 4, core % 4
        sh = res.results[core]["y"]
        y[b, 512 * g:512 * g + 512] = sh[0:512]
        y[b, 512 * (7 - g):512 * (7 - g) + 512] = sh[512:1024]
    return y
